# version 03
# baseline (speedup 1.0000x reference)
import numpy as np
import ml_dtypes
import concourse.bass as bass
import concourse.mybir as mybir
from concourse.bass_utils import run_bass_kernel_spmd
from contextlib import ExitStack

F32 = mybir.dt.float32
BF16 = mybir.dt.bfloat16
I32 = mybir.dt.int32
AF = mybir.ActivationFunctionType
ALU = mybir.AluOpType
AX = mybir.AxisListType

D = 2048
INNER = 8192
NCORE = 8
SLICE = 2048
NT = 512
NPASS = 8
NM = 4
PREF = 14336
PGRP = 14
LN_EPS = 1e-5
SQ2048 = 45.254833995939045

ENGS = ("sync", "gpsimd", "vector", "scalar", "tensor")


class _StopBuild(Exception):
    pass


class Sem:
    __slots__ = ("h", "n", "name")

    def __init__(self, h, name):
        self.h, self.n, self.name = h, 0, name


class Prog:
    def __init__(self):
        self.q = {e: [] for e in ENGS}

    def emit(self, eng, fn=None, wait=None, inc=None, inc_n=1):
        if wait is None:
            wait = []
        elif isinstance(wait, tuple):
            wait = [wait]
        flat = []
        for w in wait:
            if w is None:
                continue
            if isinstance(w, list):
                flat.extend(x for x in w if x is not None)
            else:
                flat.append(w)
        wait = flat
        for s, v in wait:
            assert v <= s.n, f"{eng}: wait {s.name}>={v} but only {s.n} emitted"
        if inc is not None:
            inc.n += inc_n
        self.q[eng].append((fn, wait, inc, inc_n))
        return (inc, inc.n) if inc is not None else None

    def replay(self, eng, b):
        for fn, waits, inc, inc_n in self.q[eng]:
            for s, v in waits:
                b.wait_ge(s.h, v)
            inst = fn(b) if fn is not None else None
            if inc is not None:
                inst.then_inc(inc.h, inc_n)


def build(n_passes=NPASS, dbg=(), stop_after=None):
    nc = bass.Bass("TRN2", target_bir_lowering=False)

    def dt(name, shape, dtyp, kind):
        return nc.dram_tensor(name, shape, dtyp, kind=kind).ap()

    x0 = dt("x0", [SLICE, D], F32, "ExternalInput")
    x1 = dt("x1", [SLICE, D], F32, "ExternalInput")
    xpref = dt("xpref", [PREF, D], F32, "ExternalInput")
    Wq_d = dt("Wq", [D, D], F32, "ExternalInput")
    Wv_d = dt("Wv", [D, D], F32, "ExternalInput")
    Wg_d = dt("Wg", [D, D], F32, "ExternalInput")
    Wo_d = dt("Wo", [D, D], F32, "ExternalInput")
    w1_d = dt("w1", [INNER, D], F32, "ExternalInput")
    w2_d = dt("w2", [D, INNER], F32, "ExternalInput")
    beta16 = dt("beta16", [16, 1], F32, "ExternalInput")
    lng_r = dt("lng_r", [1, D], F32, "ExternalInput")
    lnb_r = dt("lnb_r", [1, D], F32, "ExternalInput")
    ident_i = dt("ident", [128, 128], BF16, "ExternalInput")
    ident16_i = dt("ident16", [128, 16], BF16, "ExternalInput")
    iotacol_i = dt("iotacol", [128, 1], F32, "ExternalInput")
    onehot_i = dt("onehot", [128, PGRP], F32, "ExternalInput")
    nbound_i = dt("nbound", [128, 1], F32, "ExternalInput")

    out_d = dt("out", [2, SLICE, D], F32, "ExternalOutput")
    dbg_outs = {name: dt("dbg_" + name, shape, dty, "ExternalOutput")
                for (name, shape, dty) in dbg}

    sqv_d = dt("sqv", [16, 128, 16, 128], BF16, "Internal")
    sqq_d = dt("sqq", [16, 128, 16, 128], BF16, "Internal")
    sqg_d = dt("sqg", [16, 128, 16, 128], BF16, "Internal")
    sqo_d = dt("sqo", [8, 128, 16, 256], BF16, "Internal")
    wq1t_d = dt("wq1t", [16, 2, 128, 8, 512], BF16, "Internal")
    wq2t_d = dt("wq2t", [64, 128, 2048], BF16, "Internal")
    colscr = dt("colscr", [128, 1], F32, "Internal")
    rowscr = dt("rowscr", [16, 1], F32, "Internal")

    P = Prog()

    with ExitStack() as ctx:
        def sbt(name, shape, dtyp=F32):
            return ctx.enter_context(nc.sbuf_tensor(name, shape, dtyp))

        aA = sbt("aA", [128, 16384], BF16)
        aB = sbt("aB", [128, 16384], BF16)
        kfs = sbt("kfs", [128, 8192], BF16)
        xnf = sbt("xnf", [128, 8192], BF16)
        xping = sbt("xping", [128, 4096], F32)
        xntm = sbt("xntm", [128, 4096], BF16)
        intscr = sbt("intscr", [128, 1024], I32)
        ktms = sbt("ktms", [128, 4096], BF16)
        wr = [sbt(f"wr{i}", [128, 4096], BF16) for i in range(3)]
        prefb = sbt("prefb", [128, 4096], BF16)
        sqd = sbt("sqd", [128, 512], F32)
        ygs = sbt("ygs", [128, 512], BF16)
        ysq = sbt("ysq", [128, 512], BF16)
        mbt = sbt("mbt", [128, 512], BF16)
        rsbt = sbt("rsbt", [128, 512], BF16)
        rowsb = sbt("rowsb", [128, 1024], BF16)
        rowb2 = sbt("rowb2", [128, 2048], BF16)
        rowf = sbt("rowf", [1, 1536], F32)
        stats = sbt("stats", [128, 64], F32)
        st2 = sbt("st2", [128, 32], F32)
        pstat = sbt("pstat", [128, 64], F32)
        wcols = sbt("wcols", [128, 16], F32)
        consts = sbt("consts", [128, 8], F32)
        identsb = sbt("identsb", [128, 128], BF16)
        id16sb = sbt("id16sb", [128, 16], BF16)
        iotac = sbt("iotac", [128, 1], F32)
        ohsb = sbt("ohsb", [128, PGRP], F32)
        nbsb = sbt("nbsb", [128, 1], F32)
        lamcol = sbt("lamcol", [16, 1], F32)
        lnlamcol = sbt("lnlamcol", [16, 1], F32)
        lam1024 = sbt("lam1024", [16, 1], F32)
        pwcol = sbt("pwcol", [16, 1], F32)
        lnlbc = sbt("lnlbc", [128, 16], F32)
        lamfill = sbt("lamfill", [128, 16], F32)
        expbase = sbt("expbase", [128, 16], F32)
        expscr = sbt("expscr", [128, 16], F32)
        BASEt = sbt("BASEt", [128, 128], BF16)
        BASEs = sbt("BASEs", [128, 32], BF16)
        mb8 = sbt("mb8", [128, 8], BF16)
        lngc = sbt("lngc", [128, 16], F32)
        lnbc = sbt("lnbc", [128, 16], F32)
        zrun = sbt("zrun", [16, 2048], F32)
        z0sel = sbt("z0sel", [16, 2048], BF16)
        zb16 = sbt("zb16", [128, 2048], BF16)
        zfm = sbt("zfm", [128, 768], BF16)
        srun = sbt("srun", [16, 8], F32)
        sb16 = sbt("sb16", [128, 2], BF16)
        carry = sbt("carry", [128, 16], F32)
        C0sb = sbt("C0sb", [128, 16], F32)
        C1sb = sbt("C1sb", [128, 16], F32)
        onescol = sbt("onescol", [128, 1], BF16)
        ones1r = sbt("ones1r", [128, 128], BF16)

        psA = ctx.enter_context(nc.psum_tensor("psA", [128, 2048], F32))
        psB = ctx.enter_context(nc.psum_tensor("psB", [128, 2048], F32))

        sl = {}
        for name in ("tsm", "vsm", "ssm", "gx0", "gx1", "gk0", "gk1", "gp0", "gp1",
                     "gm", "wrs0", "wrs1", "wrs2", "sc", "so0", "so1", "sq0", "sq1",
                     "sg0", "sg1", "sdb"):
            sl[name] = Sem(ctx.enter_context(nc.semaphore(name)), name)
        tsm, vsm, ssm = sl["tsm"], sl["vsm"], sl["ssm"]
        GX = [sl["gx0"], sl["gx1"]]
        GK = [sl["gk0"], sl["gk1"]]
        GP = [sl["gp0"], sl["gp1"]]
        GM = sl["gm"]
        WRS = [sl["wrs0"], sl["wrs1"], sl["wrs2"]]
        SC = sl["sc"]
        SO = [sl["so0"], sl["so1"]]
        SQS = [sl["sq0"], sl["sq1"]]
        SG = [sl["sg0"], sl["sg1"]]
        SDB = sl["sdb"]
        store_sems = [sl["so0"], sl["so1"], sl["sq0"], sl["sq1"], sl["sg0"], sl["sg1"], sl["sdb"]]

        def _selfw(wait, sem):
            w = [(sem, sem.n)] if sem.n > 0 else []
            if wait is None:
                return w
            if isinstance(wait, tuple):
                return [wait] + w
            return list(wait) + w

        def T(fn, wait=None):
            return P.emit("tensor", fn, wait=_selfw(wait, tsm), inc=tsm)

        def T0(fn, wait=None):
            return P.emit("tensor", fn, wait=_selfw(wait, tsm), inc=None)

        def V(fn, wait=None):
            return P.emit("vector", fn, wait=_selfw(wait, vsm), inc=vsm)

        def S(fn, wait=None):
            return P.emit("scalar", fn, wait=_selfw(wait, ssm), inc=ssm)

        def GD(fn, sem, wait=None):
            return P.emit("gpsimd", fn, wait=wait, inc=sem, inc_n=16)

        def SD(fn, sem, wait=None):
            return P.emit("sync", fn, wait=wait, inc=sem, inc_n=16)

        ST = SD

        def dbg_store(name, src_ap, wait):
            if name in dbg_outs:
                ST(lambda b, s=src_ap, d=dbg_outs[name]: b.dma_start(out=d, in_=s),
                   SDB, wait=wait)

        def dbgw():
            return [(SDB, SDB.n)] if SDB.n > 0 else []

        def bank(b):
            return (psA if b < 4 else psB)[:, 512 * (b % 4):512 * (b % 4) + 512]

        bank_free = [None] * 8

        class WRing:
            def __init__(self):
                self.ready = [None] * 3
                self.free = [None] * 3
                self.srcs = []
                self.head = 0

            def queue(self, src_ap, width):
                self.srcs.append((src_ap, width))

            def _load(self):
                i = self.head
                j = i % 3
                src, w = self.srcs[i]
                q = "gpsimd" if j == 2 else "sync"
                if len(src.shape) == 3:
                    k3 = src.shape[1]
                    self.ready[j] = P.emit(
                        q,
                        lambda b, s=wr[j], a=src, ww=w, kk=k3: b.dma_start(
                            out=s[:, 0:ww].rearrange("p (k j) -> p k j", k=kk), in_=a),
                        wait=self.free[j], inc=WRS[j], inc_n=16)
                else:
                    self.ready[j] = P.emit(
                        q,
                        lambda b, s=wr[j], a=src, ww=w: b.dma_start(out=s[:, 0:ww], in_=a),
                        wait=self.free[j], inc=WRS[j], inc_n=16)
                self.head += 1

            def get(self, i, ahead=1):
                while self.head <= min(i + ahead, len(self.srcs) - 1):
                    self._load()
                return wr[i % 3], self.ready[i % 3]

            def done(self, i, semval):
                self.free[i % 3] = semval

        ring = WRing()
        ring_i = [0]

        def ring_next():
            i = ring_i[0]
            ring_i[0] += 1
            return i

        try:

            CEPS, CNH, CPH = 0, 1, 2
            V(lambda b: b.memset(consts[:, CEPS:CEPS + 1], LN_EPS))
            V(lambda b: b.memset(consts[:, CNH:CNH + 1], -0.5))
            V(lambda b: b.memset(consts[:, CPH:CPH + 1], 0.5))
            v_ones = V(lambda b: b.memset(onescol[:, :], 1.0))
            V(lambda b: b.memset(ones1r[:, :], 0.0))
            V(lambda b: b.memset(ones1r[0:1, :], 1.0))
            V(lambda b: b.memset(zb16[:, :], 0.0))
            V(lambda b: b.memset(sb16[:, :], 0.0))
            V(lambda b: b.memset(rowsb[:, :], 0.0))
            V(lambda b: b.memset(rowb2[:, :], 0.0))
            V(lambda b: b.memset(zrun[:, :], 0.0))
            V(lambda b: b.memset(z0sel[:, :], 0.0))
            v_const = V(lambda b: b.memset(srun[:, :], 0.0))

            SD(lambda b: b.dma_start(out=identsb[:, :], in_=ident_i), SC)
            SD(lambda b: b.dma_start(out=id16sb[:, :], in_=ident16_i), SC)
            SD(lambda b: b.dma_start(out=iotac[:, :], in_=iotacol_i), SC)
            SD(lambda b: b.dma_start(out=ohsb[:, :], in_=onehot_i), SC)
            SD(lambda b: b.dma_start(out=nbsb[:, :], in_=nbound_i), SC)
            SD(lambda b: b.dma_start(out=lamcol[:, :], in_=beta16), SC)
            with nc.allow_non_contiguous_dma("one-time 8KB param layout"):
                SD(lambda b: b.dma_start(out=lngc[:, :],
                                         in_=lng_r.rearrange("one (c p) -> (one p) c", p=128),
                                         allow_slow_non_contiguous=True), SC)
                SD(lambda b: b.dma_start(out=lnbc[:, :],
                                         in_=lnb_r.rearrange("one (c p) -> (one p) c", p=128),
                                         allow_slow_non_contiguous=True), SC)
            ld_consts = (SC, SC.n)

            if stop_after == 'consts':
                raise _StopBuild
            s_lam = S(lambda b: b.activation(lamcol[:, :], lamcol[:, :], AF.Sigmoid), wait=ld_consts)
            s_lnl = S(lambda b: b.activation(lnlamcol[:, :], lamcol[:, :], AF.Ln), wait=s_lam)
            v_t0 = V(lambda b: b.tensor_scalar(srun[:, 3:4], lnlamcol[:, :], 1024.0, None, ALU.mult),
                     wait=[s_lnl, v_const])
            s_1024 = S(lambda b: b.activation(lam1024[:, :], srun[:, 3:4], AF.Exp), wait=v_t0)
            v_t1 = V(lambda b: b.tensor_tensor(srun[:, 4:5], srun[:, 3:4], nbsb[0:16, 0:1], ALU.mult),
                     wait=v_t0)
            s_pw = S(lambda b: b.activation(pwcol[:, :], srun[:, 4:5], AF.Exp), wait=v_t1)
            g_r0 = GD(lambda b: b.dma_start(out=rowscr, in_=lnlamcol[:, :]), GM, wait=s_lnl)
            g_bc = GD(lambda b: b.dma_start(
                out=lnlbc[:, :],
                in_=rowscr.rearrange("p one -> one p").broadcast_to([128, 16])), GM, wait=g_r0)
            s_lf = S(lambda b: b.activation(lamfill[:, :], lnlbc[:, :], AF.Exp), wait=g_bc)
            v_eb = V(lambda b: b.tensor_scalar(expbase[:, :], lnlbc[:, :], iotac[:, 0:1], None, ALU.mult),
                     wait=[g_bc, ld_consts])
            base_all = None
            for j in range(8):
                off = float(128 * (7 - j))
                v_ej = V(lambda b, o=off: b.scalar_tensor_tensor(
                    expscr[:, :], lnlbc[:, :], o, expbase[:, :], ALU.mult, ALU.add),
                    wait=[v_eb, base_all])
                base_all = S(lambda b, jj=j: b.activation(
                    BASEt[:, 16 * jj:16 * jj + 16], expscr[:, :], AF.Exp), wait=v_ej)

            if stop_after == 'params':
                raise _StopBuild
            def xslot(i):
                return xping[:, 2048 * (i % 2):2048 * (i % 2) + 2048]

            def kslot(i):
                return ktms[:, 2048 * (i % 2):2048 * (i % 2) + 2048]

            def stg(i):
                return (xnf if i % 2 == 0 else kfs)[:, 0:2048]

            xslot_free = [None, None]
            kslot_free = [None, None]
            stg_free = [None, None]

            ksi = [0]
            sq_last = None
            colsum_ready = None
            for wsel, (W_src, Wdst, cmajor) in enumerate(
                    ((Wv_d, sqv_d, True), (Wq_d, sqq_d, True),
                     (Wg_d, sqg_d, True), (Wo_d, sqo_d, False))):
                cs_mm = None
                for t in range(16):
                    i = ksi[0]
                    ksi[0] += 1
                    ld = GD(lambda b, s=W_src, tt=t, ii=i: b.dma_start(
                        out=kslot(ii), in_=s[128 * tt:128 * tt + 128, :]),
                        GK[i % 2], wait=kslot_free[i % 2])
                    if cmajor:
                        sv = ST(lambda b, d=Wdst, tt=t, ii=i: b.dma_start(
                            out=d[:, :, tt, :].rearrange("c p j -> p c j"),
                            in_=kslot(ii).rearrange("p (c j) -> p c j", c=16)),
                            SQS[i % 2], wait=ld)
                    else:
                        sv = ST(lambda b, d=Wdst, tt=t, ii=i: b.dma_start(
                            out=d[:, :, tt, :].rearrange("n p j -> p n j"),
                            in_=kslot(ii).rearrange("p (n j) -> p n j", n=8)),
                            SQS[i % 2], wait=ld)
                    if wsel == 0:
                        for nn in range(4):
                            cs_mm = T(lambda b, ii=i, n_=nn: b.matmul(
                                psA[0:1, 512 * n_:512 * n_ + 512], onescol[:, :],
                                kslot(ii)[:, 512 * n_:512 * n_ + 512],
                                start=(ii == 0), stop=(ii == 15)),
                                wait=[ld, v_ones] if nn == 0 else None)
                        sv2 = V(lambda b: b.tensor_copy(rowb2[0:1, 0:2048], psA[0:1, 0:2048]),
                                wait=cs_mm) if t == 15 else None
                        if t == 15:
                            colsum_ready = sv2
                        kslot_free[i % 2] = [sv, cs_mm] if cs_mm else sv
                    else:
                        kslot_free[i % 2] = sv
                    sq_last = sv

            if stop_after == 'sq':
                raise _StopBuild
            SW1, E1B, SW2, E2B, WT0, WT1, WP0 = 0, 1, 2, 3, 4, 5, 8
            xai = [0]
            mean_deps = []
            for wi in range(2):
                last_abs = None
                for t in range(64):
                    i = xai[0]
                    xai[0] += 1
                    if wi == 0:
                        src = w1_d[128 * t:128 * t + 128, :]
                    else:
                        src = w2_d[128 * (t // 4):128 * (t // 4) + 128,
                                   2048 * (t % 4):2048 * (t % 4) + 2048]
                    ld = GD(lambda b, s=src, ii=i: b.dma_start(out=xslot(ii), in_=s),
                            GX[i % 2], wait=xslot_free[i % 2])
                    sv = None
                    for cch in range(4):
                        sv = S(lambda b, ii=i, cc=cch: b.activation(
                            sqd[:, :], xslot(ii)[:, 512 * cc:512 * cc + 512], AF.Abs,
                            accum_out=wcols[:, WP0 + cc:WP0 + cc + 1]),
                            wait=[ld, last_abs] if cch == 0 else None)
                    va = V(lambda b, tt=t: b.reduce_sum(
                        pstat[:, tt:tt + 1], wcols[:, WP0:WP0 + 4], AX.X), wait=sv)
                    xslot_free[i % 2] = va
                    last_abs = va
                vr = V(lambda b: b.reduce_sum(wcols[:, WT0:WT0 + 1], pstat[:, 0:64], AX.X),
                       wait=last_abs)
                g1 = GD(lambda b: b.dma_start(out=colscr, in_=wcols[:, WT0:WT0 + 1]), GM, wait=vr)
                g2 = GD(lambda b: b.dma_start(out=rowf[0:1, 0:128],
                                              in_=colscr.rearrange("p one -> one p")), GM, wait=g1)
                vr2 = V(lambda b: b.reduce_sum(rowf[0:1, 128:129], rowf[0:1, 0:128], AX.X), wait=g2)
                g3 = GD(lambda b: b.dma_start(out=colscr[0:1, :], in_=rowf[0:1, 128:129]), GM, wait=vr2)
                g4 = GD(lambda b: b.dma_start(out=wcols[:, WT1:WT1 + 1],
                                              in_=colscr[0:1, :].broadcast_to([128, 1])), GM, wait=g3)
                SWc, EBc = (SW1, E1B) if wi == 0 else (SW2, E2B)
                V(lambda b: b.tensor_scalar(wcols[:, WT1:WT1 + 1], wcols[:, WT1:WT1 + 1],
                                            1.0 / (INNER * D), 1e-5, ALU.mult, ALU.max), wait=g4)
                V(lambda b, sc=SWc: b.reciprocal(wcols[:, sc:sc + 1], wcols[:, WT1:WT1 + 1]))
                md = V(lambda b, eb=EBc: b.tensor_scalar(wcols[:, eb:eb + 1], wcols[:, WT1:WT1 + 1],
                                                         1.0, None, ALU.mult))
                mean_deps.append(md)

            if stop_after == 'absmean':
                raise _StopBuild
            int_free = [None]
            prep_t_free = None

            def prep_quant(wi):
                nonlocal prep_t_free
                SWc = SW1 if wi == 0 else SW2
                tdone = None
                for t in range(64):
                    i = xai[0]
                    xai[0] += 1
                    if wi == 0:
                        src = w1_d[128 * t:128 * t + 128, :]
                    else:
                        src = w2_d[128 * (t // 4):128 * (t // 4) + 128,
                                   2048 * (t % 4):2048 * (t % 4) + 2048]
                    ld = GD(lambda b, s=src, ii=i: b.dma_start(out=xslot(ii), in_=s),
                            GX[i % 2], wait=[xslot_free[i % 2], mean_deps[wi] if t < 2 else None])
                    q2 = None
                    for qc in range(2):
                        q1 = V(lambda b, ii=i, q_=qc: b.tensor_scalar(
                            intscr[:, :], xslot(ii)[:, 1024 * q_:1024 * q_ + 1024],
                            wcols[:, SWc:SWc + 1], None, ALU.mult),
                            wait=[ld, int_free[0]] if qc == 0 else None)
                        q2 = V(lambda b, ii=i, q_=qc: b.tensor_scalar(
                            kslot(ii)[:, 1024 * q_:1024 * q_ + 1024], intscr[:, :],
                            1.0, -1.0, ALU.min, ALU.max),
                            wait=kslot_free[i % 2] if qc == 0 else None)
                    xslot_free[i % 2] = q2
                    int_free[0] = q2
                    tmm = None
                    for c in range(16):
                        tmm = T(lambda b, ii=i, cc=c: b.matmul(
                            psB[:, 128 * cc:128 * cc + 128],
                            kslot(ii)[:, 128 * cc:128 * cc + 128], identsb[:, :],
                            start=True, stop=True),
                            wait=[q2, prep_t_free] if c == 0 else None)
                    ev = V(lambda b, ii=i: b.tensor_copy(stg(ii)[:, :], psB[:, 0:2048]),
                           wait=[tmm, stg_free[i % 2]])
                    prep_t_free = ev
                    kslot_free[i % 2] = ev
                    if wi == 0:
                        for ih in range(2):
                            sv = ST(lambda b, tt=t, ii=i, hh=ih: b.dma_start(
                                out=wq1t_d[tt // 4, hh, :, :, 128 * (tt % 4):128 * (tt % 4) + 128]
                                    .rearrange("p ic j -> p ic j"),
                                in_=stg(ii)[:, 1024 * hh:1024 * hh + 1024]
                                    .rearrange("p (ic j) -> p ic j", ic=8)),
                                SG[i % 2], wait=ev if ih == 0 else None)
                    else:
                        qr, tb = t % 4, t // 4
                        sv = ST(lambda b, q=qr, tbb=tb, ii=i: b.dma_start(
                            out=wq2t_d[16 * q:16 * q + 16, :, 128 * tbb:128 * tbb + 128]
                                .rearrange("kc p j -> p kc j"),
                            in_=stg(ii).rearrange("p (kc j) -> p kc j", kc=16)),
                            SG[i % 2], wait=ev)
                    stg_free[i % 2] = sv
                    tdone = sv
                return tdone

            w1_done = prep_quant(0)
            w2_done = prep_quant(1)

            if stop_after == 'quant':
                raise _StopBuild
            PSX, PXP, PSXX, PMEAN, PVAR, PRS, PTMP = 0, 8, 12, 20, 28, 36, 44
            zps_free = None
            sps_free = None
            prefb_free = [None, None]
            bases_free = [None, None]
            t_zmm = t_smm = None
            vz0 = vs0 = None
            prev_v2 = None
            for g in range(PGRP):
                for jp in range(4):
                    pr = []
                    for jj in range(2):
                        j = 2 * jp + jj
                        i = 8 * g + j
                        ld = GD(lambda b, ii=i: b.dma_start(
                            out=prefb[:, 2048 * (ii % 2):2048 * (ii % 2) + 2048],
                            in_=xpref[128 * ii:128 * ii + 128, :]), GP[i % 2],
                            wait=prefb_free[i % 2])
                        pslot = prefb[:, 2048 * (i % 2):2048 * (i % 2) + 2048]
                        v1 = V(lambda b, ps=pslot, j_=j: b.reduce_sum(
                            pstat[:, PSX + j_:PSX + j_ + 1], ps, AX.X), wait=ld)
                        sv = None
                        for cch in range(4):
                            sv = S(lambda b, ps=pslot, cc=cch: b.activation(
                                sqd[:, :], ps[:, 512 * cc:512 * cc + 512], AF.Square,
                                accum_out=pstat[:, PXP + cc:PXP + cc + 1]),
                                wait=[ld, prev_v2,
                                      w2_done if (g == 0 and j == 0 and cch == 0) else None]
                                if cch == 0 else None)
                        v2 = V(lambda b, j_=j: b.reduce_sum(
                            pstat[:, PSXX + j_:PSXX + j_ + 1], pstat[:, PXP:PXP + 4], AX.X), wait=sv)
                        prev_v2 = v2
                        pr.append((v1, v2))
                    j0 = 2 * jp
                    V(lambda b, s0=j0: b.tensor_scalar(
                        pstat[:, PMEAN + s0:PMEAN + s0 + 2], pstat[:, PSX + s0:PSX + s0 + 2],
                        1.0 / D, None, ALU.mult), wait=[pr[0][0], pr[0][1], pr[1][0], pr[1][1]])
                    V(lambda b, s0=j0: b.tensor_scalar(
                        pstat[:, PTMP + s0:PTMP + s0 + 2], pstat[:, PSXX + s0:PSXX + s0 + 2],
                        1.0 / D, None, ALU.mult))
                    V(lambda b, s0=j0: b.tensor_tensor(
                        pstat[:, PVAR + s0:PVAR + s0 + 2], pstat[:, PMEAN + s0:PMEAN + s0 + 2],
                        pstat[:, PMEAN + s0:PMEAN + s0 + 2], ALU.mult))
                    vv = V(lambda b, s0=j0: b.tensor_tensor(
                        pstat[:, PVAR + s0:PVAR + s0 + 2], pstat[:, PTMP + s0:PTMP + s0 + 2],
                        pstat[:, PVAR + s0:PVAR + s0 + 2], ALU.subtract))
                    sv1 = S(lambda b, s0=j0: b.activation(
                        pstat[:, PTMP + s0:PTMP + s0 + 2], pstat[:, PVAR + s0:PVAR + s0 + 2],
                        AF.Ln, bias=consts[:, CEPS:CEPS + 1]), wait=vv)
                    sv2 = S(lambda b, s0=j0: b.activation(
                        pstat[:, PRS + s0:PRS + s0 + 2], pstat[:, PTMP + s0:PTMP + s0 + 2],
                        AF.Exp, scale=consts[:, CNH:CNH + 1]))
                    vmb = V(lambda b, s0=j0: b.tensor_copy(
                        mb8[:, s0:s0 + 2], pstat[:, PMEAN + s0:PMEAN + s0 + 2]), wait=sv2)
                    for jj in range(2):
                        j = 2 * jp + jj
                        i = 8 * g + j
                        vbs = V(lambda b, j_=j, ii=i: b.tensor_scalar(
                            BASEs[:, 16 * (ii % 2):16 * (ii % 2) + 16],
                            BASEt[:, 16 * j_:16 * j_ + 16],
                            pstat[:, PRS + j_:PRS + j_ + 1], None, ALU.mult),
                            wait=[sv2, bases_free[i % 2],
                                  base_all if (g == 0 and j < 2) else None])
                        pslot = prefb[:, 2048 * (i % 2):2048 * (i % 2) + 2048]
                        for nn in range(4):
                            t_zmm = T(lambda b, ii=i, n_=nn, j_=j: b.matmul(
                                psB[0:16, 512 * n_:512 * n_ + 512],
                                BASEs[:, 16 * (ii % 2):16 * (ii % 2) + 16],
                                prefb[:, 2048 * (ii % 2) + 512 * n_:2048 * (ii % 2) + 512 * n_ + 512],
                                start=(j_ == 0), stop=(j_ == 7)),
                                wait=([vbs, zps_free, prep_t_free if g == 0 else None]
                                      if (j == 0 and nn == 0) else (vbs if nn == 0 else None)))
                        t_smm = T(lambda b, ii=i, j_=j: b.matmul(
                            psA[0:16, 512:513],
                            BASEs[:, 16 * (ii % 2):16 * (ii % 2) + 16],
                            mb8[:, j_:j_ + 1],
                            start=(j_ == 0), stop=(j_ == 7)),
                            wait=[vmb, sps_free if j == 0 else None,
                                  colsum_ready if (g == 0 and j == 0) else None])
                        bases_free[i % 2] = t_smm
                        prefb_free[i % 2] = t_smm
                vch = V(lambda b: b.scalar_tensor_tensor(
                    zrun[:, :], zrun[:, :], lam1024[:, 0:1], psB[0:16, 0:2048], ALU.mult, ALU.add),
                    wait=[t_zmm, s_1024])
                zps_free = vch
                vz0 = V(lambda b, gg=g: b.scalar_tensor_tensor(
                    z0sel[:, :], zrun[:, :], ohsb[0:16, gg:gg + 1], z0sel[:, :], ALU.mult, ALU.add))
                vsch = V(lambda b: b.scalar_tensor_tensor(
                    srun[:, 0:1], srun[:, 0:1], lam1024[:, 0:1], psA[0:16, 512:513],
                    ALU.mult, ALU.add), wait=t_smm)
                sps_free = vsch
                vs0 = V(lambda b, gg=g: b.scalar_tensor_tensor(
                    srun[:, 1:2], srun[:, 0:1], ohsb[0:16, gg:gg + 1], srun[:, 1:2],
                    ALU.mult, ALU.add))

            if stop_after == 'chain':
                raise _StopBuild
            v_zc = V(lambda b: b.tensor_copy(zb16[0:16, :], z0sel[:, :]), wait=[vz0, zps_free])
            if stop_after == "zc":
                raise _StopBuild
            tzl = []
            for c in range(16):
                em = T if c == 15 else T0
                tzl.append(em(lambda b, cc=c: b.matmul(
                    psB[:, 16 * cc:16 * cc + 16], zb16[:, 128 * cc:128 * cc + 128],
                    id16sb[:, :], start=True, stop=True), wait=v_zc if c == 0 else None))
            V(lambda b: b.memset(zfm[:, :], 0.0), wait=None)
            v_zf = None
            for c in range(16):
                v_zf = V(lambda b, cc=c: b.tensor_copy(
                    zfm[:, 48 * cc:48 * cc + 16], psB[:, 16 * cc:16 * cc + 16]),
                    wait=tzl[-1] if c == 0 else None)
            if stop_after == "zf1":
                raise _StopBuild
            v_z1 = V(lambda b: b.tensor_scalar(z0sel[:, :], z0sel[:, :], pwcol[:, 0:1], None, ALU.mult),
                     wait=[v_zf, s_pw])
            v_z1 = V(lambda b: b.tensor_tensor(z0sel[:, :], zrun[:, :], z0sel[:, :], ALU.subtract))
            v_zc1 = V(lambda b: b.tensor_copy(zb16[0:16, :], z0sel[:, :]))
            tzl = []
            for c in range(16):
                em = T if c == 15 else T0
                tzl.append(em(lambda b, cc=c: b.matmul(
                    psB[:, 16 * cc:16 * cc + 16], zb16[:, 128 * cc:128 * cc + 128],
                    id16sb[:, :], start=True, stop=True), wait=v_zc1 if c == 0 else None))
            for c in range(16):
                v_zf = V(lambda b, cc=c: b.tensor_copy(
                    zfm[:, 48 * cc + 32:48 * cc + 48], psB[:, 16 * cc:16 * cc + 16]),
                    wait=tzl[-1] if c == 0 else None)
            zfm_done = v_zf
            if stop_after == 'zfm':
                raise _StopBuild
            v_s1 = V(lambda b: b.tensor_scalar(srun[:, 2:3], srun[:, 1:2], pwcol[:, 0:1], None, ALU.mult),
                     wait=[vs0, s_pw, sps_free])
            v_s1 = V(lambda b: b.tensor_tensor(srun[:, 2:3], srun[:, 0:1], srun[:, 2:3], ALU.subtract))
            V(lambda b: b.tensor_scalar(sb16[0:16, 0:1], srun[:, 1:2], -1.0, None, ALU.mult))
            v_sn = V(lambda b: b.tensor_scalar(sb16[0:16, 1:2], srun[:, 2:3], -1.0, None, ALU.mult))
            t_sr = T0(lambda b: b.matmul(psB[0:1, 1024:1040], sb16[:, 0:1], id16sb[:, :],
                                         start=True, stop=True), wait=v_sn)
            t_sr = T(lambda b: b.matmul(psB[0:1, 1040:1056], sb16[:, 1:2], id16sb[:, :],
                                        start=True, stop=True))
            v_srow = V(lambda b: b.memset(rowsb[0:1, 0:48], 0.0))
            v_srow = V(lambda b: b.tensor_copy(rowsb[0:1, 0:16], psB[0:1, 1024:1040]), wait=t_sr)
            v_srow = V(lambda b: b.tensor_copy(rowsb[0:1, 32:48], psB[0:1, 1040:1056]))

            if stop_after == 'snegs':
                raise _StopBuild
            wvp_free = [prefb_free[0], prefb_free[1]]
            g_sw = None
            cs_mm = None
            for c in range(16):
                ld = GD(lambda b, cc=c, ii=c: b.dma_start(
                    out=prefb[:, 2048 * (ii % 2):2048 * (ii % 2) + 2048],
                    in_=Wv_d[128 * cc:128 * cc + 128, :]), GP[c % 2], wait=wvp_free[c % 2])
                for nn in range(4):
                    g_sw = T(lambda b, cc=c, n_=nn: b.matmul(
                        psA[0:48, 512 * n_:512 * n_ + 512],
                        zfm[:, 48 * cc:48 * cc + 48],
                        prefb[:, 2048 * (cc % 2) + 512 * n_:2048 * (cc % 2) + 512 * n_ + 512],
                        start=(cc == 0), stop=False),
                        wait=([ld, zfm_done, sps_free] if (c == 0 and nn == 0)
                              else (ld if nn == 0 else None)))
                wvp_free[c % 2] = g_sw
            if stop_after == 'gsweep':
                raise _StopBuild
            v_cs = colsum_ready
            rk = None
            for nn in range(4):
                rk = T(lambda b, n_=nn: b.matmul(
                    psA[0:48, 512 * n_:512 * n_ + 512], rowsb[:, 0:48],
                    rowb2[:, 512 * n_:512 * n_ + 512], start=False, stop=True),
                    wait=[v_cs, v_srow, g_sw] if nn == 0 else None)
            if stop_after == 'rank1':
                raise _StopBuild
            v_cx = None
            for bsel in range(2):
                if bsel == 1:
                    V(lambda b: b.memset(zb16[0:16, :], 0.0), wait=v_cx)
                v_gb = V(lambda b, bs=bsel: b.tensor_copy(
                    zb16[32 * bs:32 * bs + 16, :], psA[32 * bs:32 * bs + 16, 0:2048]),
                    wait=rk if bsel == 0 else v_cx)
                txl = []
                for c in range(16):
                    em = T if c == 15 else T0
                    txl.append(em(lambda b, cc=c: b.matmul(
                        psB[:, 16 * cc:16 * cc + 16], zb16[:, 128 * cc:128 * cc + 128],
                        id16sb[:, :], start=True, stop=True),
                        wait=v_gb if c == 0 else None))
                Cdst = C0sb if bsel == 0 else C1sb
                v_cx = V(lambda b, d=Cdst: b.tensor_copy(d[:, :], psB[:, 0:272:17]), wait=txl[-1])
            prologue_done = v_cx
            dbg_store("C0", C0sb[:, :], prologue_done)
            dbg_store("C1", C1sb[:, :], prologue_done)
            if stop_after == "prologue":
                raise _StopBuild

            for p in range(n_passes):
                for mp in range(2):
                    for np_ in range(16):
                        for ih in range(2):
                            ring.queue(wq1t_d[np_, ih].rearrange("p ic j -> p (ic j)"), 4096)
                    for u in range(32):
                        ring.queue(wq2t_d[2 * u:2 * u + 2].rearrange("k p j -> p k j"), 4096)
                for W in (sqv_d, sqq_d, sqg_d):
                    for c in range(16):
                        ring.queue(W[c].rearrange("p ic j -> p (ic j)"), 2048)
                for n in range(8):
                    ring.queue(sqo_d[n].rearrange("p dc j -> p (dc j)"), 4096)

            SX, SXXP, SXX, MEAN, VAR, RS, ALPH, MX, ASC, E1C, TM0, TM1, NRM = \
                0, 4, 20, 24, 28, 32, 36, 40, 44, 48, 52, 56, 60
            FSH, FMX, FAS, FE2, P8, FT0, FT1, FT2, FNR = 0, 4, 8, 12, 16, 24, 26, 28, 30

            xp_free = [xslot_free[0], xslot_free[1]]
            xn_free = [None, None]
            kt_free = [kslot_free[0], kslot_free[1]]
            int_f = int_free[0]
            xnf_free = stg_free[0]
            kfs_free = stg_free[1]
            aA_free = None
            aB_free = None
            sqd_free = None
            carry_init = None
            tr_free = prep_t_free

            def xnf_dst(m, grp):
                return xnf.rearrange("p (c mm j) -> p c mm j", c=16, mm=4)[:, 4 * grp:4 * grp + 4, m, :]

            def kfs_dst(m, grp):
                return kfs.rearrange("p (c mm j) -> p c mm j", c=16, mm=4)[:, 4 * grp:4 * grp + 4, m, :]

            pA_free = rk
            pB_free = prologue_done

            for p in range(n_passes):
                bsel = p // 4
                xb = x0 if bsel == 0 else x1
                r0 = (p % 4) * NT

                v_apply = [None] * NM
                ph1_ev = None
                for m in range(NM):
                    ld = GD(lambda b, mm=m, xb_=xb, r_=r0: b.dma_start(
                        out=xslot(mm), in_=xb_[r_ + 128 * mm:r_ + 128 * mm + 128, :]),
                        GX[m % 2],
                        wait=[xp_free[m % 2], prologue_done if (p == 0 and m == 0) else None])
                    xs = xslot(m)
                    v1 = V(lambda b, x_=xs, mm=m: b.reduce_sum(
                        stats[:, SX + mm:SX + mm + 1], x_, AX.X), wait=ld)
                    sv = None
                    for cch in range(4):
                        sv = S(lambda b, x_=xs, cc=cch, mm=m: b.activation(
                            sqd[:, :], x_[:, 512 * cc:512 * cc + 512], AF.Square,
                            accum_out=stats[:, SXXP + 4 * mm + cc:SXXP + 4 * mm + cc + 1]),
                            wait=[ld, sqd_free] if cch == 0 else None)
                    sqd_free = sv
                    v2 = V(lambda b, mm=m: b.reduce_sum(
                        stats[:, SXX + mm:SXX + mm + 1],
                        stats[:, SXXP + 4 * mm:SXXP + 4 * mm + 4], AX.X), wait=sv)
                    V(lambda b, mm=m: b.tensor_scalar(
                        stats[:, MEAN + mm:MEAN + mm + 1], stats[:, SX + mm:SX + mm + 1],
                        1.0 / D, None, ALU.mult), wait=[v1, v2])
                    V(lambda b, mm=m: b.tensor_scalar(
                        stats[:, TM0 + mm:TM0 + mm + 1], stats[:, SXX + mm:SXX + mm + 1],
                        1.0 / D, None, ALU.mult))
                    V(lambda b, mm=m: b.tensor_tensor(
                        stats[:, VAR + mm:VAR + mm + 1], stats[:, MEAN + mm:MEAN + mm + 1],
                        stats[:, MEAN + mm:MEAN + mm + 1], ALU.mult))
                    vv = V(lambda b, mm=m: b.tensor_tensor(
                        stats[:, VAR + mm:VAR + mm + 1], stats[:, TM0 + mm:TM0 + mm + 1],
                        stats[:, VAR + mm:VAR + mm + 1], ALU.subtract))
                    s1 = S(lambda b, mm=m: b.activation(
                        stats[:, TM0 + mm:TM0 + mm + 1], stats[:, VAR + mm:VAR + mm + 1],
                        AF.Ln, bias=consts[:, CEPS:CEPS + 1]), wait=vv)
                    s2 = S(lambda b, mm=m: b.activation(
                        stats[:, RS + mm:RS + mm + 1], stats[:, TM0 + mm:TM0 + mm + 1],
                        AF.Exp, scale=consts[:, CNH:CNH + 1]))
                    V(lambda b, mm=m: b.tensor_tensor(
                        stats[:, TM1 + mm:TM1 + mm + 1], stats[:, RS + mm:RS + mm + 1],
                        stats[:, RS + mm:RS + mm + 1], ALU.mult), wait=s2)
                    vv = V(lambda b, mm=m: b.scalar_tensor_tensor(
                        stats[:, NRM + mm:NRM + mm + 1], stats[:, VAR + mm:VAR + mm + 1],
                        float(D), stats[:, TM1 + mm:TM1 + mm + 1], ALU.mult, ALU.mult))
                    s3 = S(lambda b, mm=m: b.activation(
                        stats[:, NRM + mm:NRM + mm + 1], stats[:, NRM + mm:NRM + mm + 1],
                        AF.Ln), wait=vv)
                    s4 = S(lambda b, mm=m: b.activation(
                        stats[:, NRM + mm:NRM + mm + 1], stats[:, NRM + mm:NRM + mm + 1],
                        AF.Exp, scale=consts[:, CPH:CPH + 1]))
                    V(lambda b, mm=m: b.tensor_scalar(
                        stats[:, NRM + mm:NRM + mm + 1], stats[:, NRM + mm:NRM + mm + 1],
                        1e-12, None, ALU.max), wait=s4)
                    V(lambda b, mm=m: b.reciprocal(
                        stats[:, TM1 + mm:TM1 + mm + 1], stats[:, NRM + mm:NRM + mm + 1]))
                    va = V(lambda b, mm=m: b.tensor_scalar(
                        stats[:, ALPH + mm:ALPH + mm + 1], stats[:, TM1 + mm:TM1 + mm + 1],
                        SQ2048, None, ALU.mult))
                    v_apply[m] = V(lambda b, x_=xs, mm=m: b.tensor_scalar(
                        xntm[:, 2048 * (mm % 2):2048 * (mm % 2) + 2048], x_,
                        stats[:, MEAN + mm:MEAN + mm + 1], stats[:, RS + mm:RS + mm + 1],
                        ALU.subtract, ALU.mult), wait=[s2, xn_free[m % 2]])
                    xp_free[m % 2] = v_apply[m]
                    xnslot = xntm[:, 2048 * (m % 2):2048 * (m % 2) + 2048]
                    vmx = V(lambda b, x_=xnslot, mm=m: b.reduce_max(
                        stats[:, MX + mm:MX + mm + 1], x_, AX.X, apply_absolute_value=True))
                    V(lambda b, mm=m: b.tensor_tensor(
                        stats[:, TM0 + mm:TM0 + mm + 1], stats[:, ALPH + mm:ALPH + mm + 1],
                        stats[:, MX + mm:MX + mm + 1], ALU.mult), wait=[va, vmx])
                    V(lambda b, mm=m: b.tensor_scalar(
                        stats[:, TM0 + mm:TM0 + mm + 1], stats[:, TM0 + mm:TM0 + mm + 1],
                        1e-5, None, ALU.max))
                    V(lambda b, mm=m: b.reciprocal(
                        stats[:, TM1 + mm:TM1 + mm + 1], stats[:, TM0 + mm:TM0 + mm + 1]))
                    V(lambda b, mm=m: b.scalar_tensor_tensor(
                        stats[:, ASC + mm:ASC + mm + 1], stats[:, ALPH + mm:ALPH + mm + 1],
                        127.0, stats[:, TM1 + mm:TM1 + mm + 1], ALU.mult, ALU.mult))
                    V(lambda b, mm=m: b.scalar_tensor_tensor(
                        stats[:, E1C + mm:E1C + mm + 1], stats[:, TM0 + mm:TM0 + mm + 1],
                        1.0 / 127.0, wcols[:, E1B:E1B + 1], ALU.mult, ALU.mult))
                    vq2 = None
                    for qc in range(2):
                        V(lambda b, x_=xnslot, mm=m, q_=qc: b.tensor_scalar(
                            intscr[:, :], x_[:, 1024 * q_:1024 * q_ + 1024],
                            stats[:, ASC + mm:ASC + mm + 1], None, ALU.mult),
                            wait=int_f if qc == 0 else None)
                        vq2 = V(lambda b, mm=m, q_=qc: b.tensor_copy(
                            kslot(mm)[:, 1024 * q_:1024 * q_ + 1024], intscr[:, :]),
                            wait=kt_free[m % 2] if qc == 0 else None)
                    int_f = vq2
                    for src_kind in range(2):
                        srcslot = xnslot if src_kind == 0 else kslot(m)
                        dep = v_apply[m] if src_kind == 0 else vq2
                        for grp in range(4):
                            tmm = None
                            for c4 in range(4):
                                c = 4 * grp + c4
                                tmm = T(lambda b, s_=srcslot, cc=c, g4=c4: b.matmul(
                                    psB[:, 512 + 128 * g4:512 + 128 * g4 + 128],
                                    s_[:, 128 * cc:128 * cc + 128], identsb[:, :],
                                    start=True, stop=True),
                                    wait=[dep, tr_free] if c4 == 0 else None)
                            if src_kind == 0:
                                ev = V(lambda b, mm=m, g4=grp: b.tensor_copy(
                                    xnf_dst(mm, g4),
                                    psB[:, 512:1024].rearrange("p (c j) -> p c j", c=4)),
                                    wait=[tmm, xnf_free if (m == 0 and grp == 0) else None])
                            else:
                                ev = V(lambda b, mm=m, g4=grp: b.tensor_copy(
                                    kfs_dst(mm, g4),
                                    psB[:, 512:1024].rearrange("p (c j) -> p c j", c=4)),
                                    wait=[tmm, kfs_free if (m == 0 and grp == 0) else None])
                            tr_free = ev
                    kt_free[m % 2] = ev
                    xn_free[m % 2] = ev
                    ph1_ev = ev
                xnf_ready = ph1_ev
                kfs_ready = ph1_ev
                dbg_store(f"xnf{p}", xnf[:, :], xnf_ready)
                dbg_store(f"kfs{p}", kfs[:, :], kfs_ready)
                dbg_store(f"stats{p}", stats[:, :], xnf_ready)

                g_evac = [None] * NM
                for mp in range(2):
                    h_ev = [None, None]
                    for np_ in range(16):
                        i0 = ring_next()
                        sl0, rd0 = ring.get(i0)
                        i1 = ring_next()
                        sl1, rd1 = ring.get(i1)
                        for mr in range(2):
                            m = 2 * mp + mr
                            bslot = bank(2 * mr + (np_ % 2))
                            tmm = None
                            for ih in range(2):
                                slx, rdx = (sl0, rd0) if ih == 0 else (sl1, rd1)
                                for ic8 in range(8):
                                    ic = 8 * ih + ic8
                                    tmm = T(lambda b, s_=slx, mm=m, icc=ic, i8=ic8, bk=bslot: b.matmul(
                                        bk, kfs[:, 512 * icc + 128 * mm:512 * icc + 128 * mm + 128],
                                        s_[:, 512 * i8:512 * i8 + 512],
                                        start=(icc == 0), stop=(icc == 15)),
                                        wait=([rdx, kfs_ready if (np_ == 0 and mr == 0) else None,
                                               bank_free[2 * mr + (np_ % 2)],
                                               pA_free if (np_ < 2 and mp == 0) else None]
                                              if ic8 == 0 else None))
                            hs = S(lambda b, mm=m, mr_=mr, n_=np_, bk=bslot: b.activation(
                                aA[:, 8192 * mr_ + 512 * n_:8192 * mr_ + 512 * n_ + 512],
                                bk, AF.Gelu, scale=stats[:, E1C + mm:E1C + mm + 1]),
                                wait=[tmm, aA_free if (mp == 0 and np_ == 0) else None])
                            bank_free[2 * mr + (np_ % 2)] = hs
                            h_ev[mr] = hs
                        ring.done(i0, (tsm, tsm.n))
                        ring.done(i1, (tsm, tsm.n))
                    if mp == 0:
                        dbg_store(f"h1_{p}", aA[:, :], [h_ev[0], h_ev[1]])
                    ffn2_dep = [None, None]
                    for mr in range(2):
                        m = 2 * mp + mr
                        hrow = aA[:, 8192 * mr:8192 * mr + 8192]
                        sg2 = S(lambda b, h_=hrow: b.activation(h_, h_, AF.Gelu), wait=[h_ev[mr]] + dbgw())
                        vmx2 = V(lambda b, h_=hrow, mm=m: b.reduce_max(
                            st2[:, FMX + mm:FMX + mm + 1], h_, AX.X, apply_absolute_value=True),
                            wait=sg2)
                        sv = None
                        for cch in range(16):
                            dstc = (st2[:, P8 + cch:P8 + cch + 1] if cch < 8
                                    else pstat[:, cch - 8:cch - 8 + 1])
                            sv = S(lambda b, h_=hrow, cc=cch, dc=dstc: b.activation(
                                sqd[:, :], h_[:, 512 * cc:512 * cc + 512],
                                AF.Square, accum_out=dc), wait=[sg2] if cch == 0 else None)
                        vred = V(lambda b, mm=m: b.reduce_sum(
                            st2[:, FT0 + mm % 2:FT0 + mm % 2 + 1], st2[:, P8:P8 + 8], AX.X),
                            wait=sv)
                        V(lambda b, mm=m: b.reduce_sum(
                            st2[:, FT1 + mm % 2:FT1 + mm % 2 + 1], pstat[:, 0:8], AX.X))
                        vred = V(lambda b, mm=m: b.tensor_tensor(
                            st2[:, FSH + mm:FSH + mm + 1], st2[:, FT0 + mm % 2:FT0 + mm % 2 + 1],
                            st2[:, FT1 + mm % 2:FT1 + mm % 2 + 1], ALU.add))
                        if mp == 0 and mr == 0:
                            dbg_store(f"h_{p}", aA[:, 0:8192], sg2)
                        vv = V(lambda b, mm=m: b.tensor_scalar(
                            st2[:, FNR + mm % 2:FNR + mm % 2 + 1], st2[:, FSH + mm:FSH + mm + 1],
                            1e-24, None, ALU.max), wait=[vmx2, vred])
                        s5 = S(lambda b, mm=m: b.activation(
                            st2[:, FNR + mm % 2:FNR + mm % 2 + 1], st2[:, FNR + mm % 2:FNR + mm % 2 + 1],
                            AF.Ln), wait=vv)
                        s6 = S(lambda b, mm=m: b.activation(
                            st2[:, FNR + mm % 2:FNR + mm % 2 + 1], st2[:, FNR + mm % 2:FNR + mm % 2 + 1],
                            AF.Exp, scale=consts[:, CPH:CPH + 1]))
                        V(lambda b, mm=m: b.tensor_scalar(
                            st2[:, FNR + mm % 2:FNR + mm % 2 + 1], st2[:, FNR + mm % 2:FNR + mm % 2 + 1],
                            1e-12, None, ALU.max), wait=s6)
                        V(lambda b, mm=m: b.reciprocal(
                            st2[:, FT0 + mm % 2:FT0 + mm % 2 + 1], st2[:, FNR + mm % 2:FNR + mm % 2 + 1]))
                        va2 = V(lambda b, mm=m: b.tensor_scalar(
                            st2[:, FT0 + mm % 2:FT0 + mm % 2 + 1], st2[:, FT0 + mm % 2:FT0 + mm % 2 + 1],
                            float(np.sqrt(INNER)), None, ALU.mult))
                        V(lambda b, mm=m: b.tensor_tensor(
                            st2[:, FT1 + mm % 2:FT1 + mm % 2 + 1], st2[:, FT0 + mm % 2:FT0 + mm % 2 + 1],
                            st2[:, FMX + mm:FMX + mm + 1], ALU.mult), wait=va2)
                        V(lambda b, mm=m: b.tensor_scalar(
                            st2[:, FT1 + mm % 2:FT1 + mm % 2 + 1], st2[:, FT1 + mm % 2:FT1 + mm % 2 + 1],
                            1e-5, None, ALU.max))
                        V(lambda b, mm=m: b.reciprocal(
                            st2[:, FT2 + mm % 2:FT2 + mm % 2 + 1], st2[:, FT1 + mm % 2:FT1 + mm % 2 + 1]))
                        V(lambda b, mm=m: b.scalar_tensor_tensor(
                            st2[:, FAS + mm:FAS + mm + 1], st2[:, FT0 + mm % 2:FT0 + mm % 2 + 1],
                            127.0, st2[:, FT2 + mm % 2:FT2 + mm % 2 + 1], ALU.mult, ALU.mult))
                        ve2 = V(lambda b, mm=m: b.scalar_tensor_tensor(
                            st2[:, FE2 + mm:FE2 + mm + 1], st2[:, FT1 + mm % 2:FT1 + mm % 2 + 1],
                            1.0 / 127.0, wcols[:, E2B:E2B + 1], ALU.mult, ALU.mult))
                        q_last = None
                        for cch in range(8):
                            V(lambda b, h_=hrow, cc=cch, mm=m: b.tensor_scalar(
                                intscr[:, :], h_[:, 1024 * cc:1024 * cc + 1024],
                                st2[:, FAS + mm:FAS + mm + 1], None, ALU.mult),
                                wait=[ve2, int_f] + dbgw() if cch == 0 else None)
                            q_last = V(lambda b, h_=hrow, cc=cch: b.tensor_copy(
                                h_[:, 1024 * cc:1024 * cc + 1024], intscr[:, :]))
                            int_f = q_last
                        ffn2_dep[mr] = q_last
                    ktr = None
                    for kc2 in range(32):
                        tmm = None
                        for sub in range(4):
                            kc = 2 * kc2 + sub // 2
                            mr = sub % 2
                            tmm = T(lambda b, kcc=kc, mr_=mr, s4=sub: b.matmul(
                                psB[:, 512 + 128 * s4:512 + 128 * s4 + 128],
                                aA[:, 8192 * mr_ + 128 * kcc:8192 * mr_ + 128 * kcc + 128],
                                identsb[:, :], start=True, stop=True),
                                wait=([ffn2_dep[0], ffn2_dep[1], tr_free] if (kc2 == 0 and sub == 0)
                                      else (tr_free if sub == 0 else None)))
                        ktr = V(lambda b, k2=kc2: b.tensor_copy(
                            aB[:, 512 * k2:512 * k2 + 512],
                            psB[:, 512:1024]),
                            wait=[tmm, aB_free if kc2 == 0 else None])
                        tr_free = ktr
                    o_mm = None
                    for u in range(32):
                        i2 = ring_next()
                        sl2, rd2 = ring.get(i2)
                        for kk in range(2):
                            kc = 2 * u + kk
                            for mr in range(2):
                                psX = psA if mr == 0 else psB
                                for n in range(4):
                                    o_mm = T(lambda b, kcc=kc, mr_=mr, n_=n, s_=sl2, k_=kk, px=psX: b.matmul(
                                        px[:, 512 * n_:512 * n_ + 512],
                                        aB[:, 256 * kcc + 128 * mr_:256 * kcc + 128 * mr_ + 128],
                                        s_[:, 2048 * k_ + 512 * n_:2048 * k_ + 512 * n_ + 512],
                                        start=(kcc == 0), stop=(kcc == 63)),
                                        wait=([rd2, ktr,
                                               pA_free if mr == 0 else pB_free,
                                               bank_free[0], bank_free[1], bank_free[2], bank_free[3],
                                               tr_free]
                                              if (u == 0 and kk == 0 and n == 0)
                                              else (rd2 if (kk == 0 and mr == 0 and n == 0)
                                                    else None)))
                        ring.done(i2, (tsm, tsm.n))
                    for mr in range(2):
                        m = 2 * mp + mr
                        psX = psA if mr == 0 else psB
                        gdst = ktms if m < 2 else kfs
                        gcol = 2048 * m if m < 2 else 2048 * m
                        g_evac[m] = S(lambda b, px=psX, gd=gdst, gc=gcol, mm=m: b.activation(
                            gd[:, gc:gc + 2048], px[:, 0:2048],
                            AF.Silu, scale=st2[:, FE2 + mm:FE2 + mm + 1]),
                            wait=[o_mm, kt_free[mr] if m < 2 else None] + dbgw())
                    pA_free = g_evac[2 * mp + 0]
                    pB_free = g_evac[2 * mp + 1]
                    for b4 in range(4):
                        bank_free[b4] = g_evac[2 * mp + 0]
                    aA_free = ktr
                    aB_free = o_mm
                ffn_done = g_evac[3]
                dbg_store(f"g{p}", kfs[:, :], ffn_done)

                if p == 0 or p == 4:
                    Csrc = C0sb if p == 0 else C1sb
                    carry_init = V(lambda b, cs=Csrc: b.tensor_copy(carry[:, :], cs[:, :]),
                                   wait=[prologue_done, carry_init])
                v_ev = [None] * 16
                scan_done = None
                for c in range(16):
                    i3 = ring_next()
                    sl3, rd3 = ring.get(i3)
                    bnk = c % 2
                    tmm = None
                    for kc in range(16):
                        tmm = T(lambda b, s_=sl3, cc=c, k_=kc: b.matmul(
                            bank(cc % 2), s_[:, 128 * k_:128 * k_ + 128],
                            xnf[:, 512 * k_:512 * k_ + 512],
                            start=(k_ == 0), stop=(k_ == 15)),
                            wait=([rd3, xnf_ready if c == 0 else None, bank_free[c % 2],
                                   aA_free if c < 2 else None,
                                   pA_free if c < 2 else None] if kc == 0 else None))
                    ring.done(i3, (tsm, tsm.n))
                    v_ev[c] = V(lambda b, cc=c: b.tensor_copy(
                        aA[:, 512 * cc:512 * cc + 512], bank(cc % 2)), wait=tmm)
                    bank_free[c % 2] = v_ev[c]
                    scan_done = V(lambda b, cc=c: b.tensor_tensor_scan(
                        aA[:, 8192 + 512 * cc:8192 + 512 * cc + 512],
                        lamfill[:, cc:cc + 1].broadcast_to([128, 512]),
                        aA[:, 512 * cc:512 * cc + 512],
                        carry[:, cc:cc + 1], ALU.mult, ALU.add),
                        wait=[carry_init, s_lf] if c == 0 else None)
                v_carry = V(lambda b: b.tensor_copy(carry[:, :], aA[:, 8192 + 511::512]),
                            wait=scan_done)
                carry_init = v_carry
                dbg_store(f"S{p}", aA[:, 8192:16384], v_carry)

                q_ev = [None] * 16
                for c in range(16):
                    i4 = ring_next()
                    sl4, rd4 = ring.get(i4)
                    tmm = None
                    for kc in range(16):
                        tmm = T(lambda b, s_=sl4, cc=c, k_=kc: b.matmul(
                            bank(cc % 2), s_[:, 128 * k_:128 * k_ + 128],
                            xnf[:, 512 * k_:512 * k_ + 512],
                            start=(k_ == 0), stop=(k_ == 15)),
                            wait=[rd4, bank_free[c % 2], aB_free if c < 2 else None]
                            if kc == 0 else None)
                    ring.done(i4, (tsm, tsm.n))
                    q_ev[c] = V(lambda b, cc=c: b.tensor_copy(
                        aB[:, 512 * cc:512 * cc + 512], bank(cc % 2)), wait=tmm)
                    bank_free[c % 2] = q_ev[c]
                y_done = [None] * 16
                for c in range(16):
                    y_done[c] = V(lambda b, cc=c: b.tensor_tensor(
                        aB[:, 512 * cc:512 * cc + 512], aB[:, 512 * cc:512 * cc + 512],
                        aA[:, 8192 + 512 * cc:8192 + 512 * cc + 512], ALU.mult),
                        wait=[q_ev[c], scan_done if c == 0 else None])
                dbg_store(f"y{p}", aB[:, 0:8192], y_done[-1])

                gate_ev = [None] * 16
                for c in range(16):
                    i5 = ring_next()
                    sl5, rd5 = ring.get(i5)
                    tmm = None
                    for kc in range(16):
                        tmm = T(lambda b, s_=sl5, cc=c, k_=kc: b.matmul(
                            bank(cc % 2), s_[:, 128 * k_:128 * k_ + 128],
                            xnf[:, 512 * k_:512 * k_ + 512],
                            start=(k_ == 0), stop=(k_ == 15)),
                            wait=[rd5, bank_free[c % 2], y_done[c] if c < 16 else None]
                            if kc == 0 else None)
                    ring.done(i5, (tsm, tsm.n))
                    gate_ev[c] = S(lambda b, cc=c: b.activation(
                        aA[:, 8192 + 512 * cc:8192 + 512 * cc + 512], bank(cc % 2), AF.Silu),
                        wait=[tmm] + (dbgw() if c == 0 else []))
                    bank_free[c % 2] = gate_ev[c]

                t_sy = None
                for c in range(16):
                    t_sy = T(lambda b, cc=c: b.matmul(
                        psB[0:1, 0:512], onescol[:, :], aB[:, 512 * cc:512 * cc + 512],
                        start=(cc == 0), stop=(cc == 15)),
                        wait=[y_done[15], tr_free, pB_free] if c == 0 else None)
                sy2_dep = None
                for c in range(16):
                    sq_s = S(lambda b, cc=c: b.activation(
                        ysq[:, :], aB[:, 512 * cc:512 * cc + 512], AF.Square),
                        wait=[y_done[15], sqd_free] if c == 0 else sy2_dep)
                    sqd_free = sq_s
                    sy2_dep = T(lambda b, cc=c: b.matmul(
                        psB[32:33, 0:512], onescol[:, :], ysq[:, :],
                        start=(cc == 0), stop=(cc == 15)), wait=sq_s)
                v_r1 = V(lambda b: b.tensor_scalar(rowf[0:1, 0:512], psB[0:1, 0:512],
                                                   1.0 / D, None, ALU.mult), wait=[t_sy, sy2_dep])
                v_r2 = V(lambda b: b.tensor_scalar(rowf[0:1, 512:1024], psB[32:33, 0:512],
                                                   1.0 / D, None, ALU.mult))
                v_r3 = V(lambda b: b.tensor_tensor(rowf[0:1, 1024:1536], rowf[0:1, 0:512],
                                                   rowf[0:1, 0:512], ALU.mult))
                v_r4 = V(lambda b: b.tensor_tensor(rowf[0:1, 512:1024], rowf[0:1, 512:1024],
                                                   rowf[0:1, 1024:1536], ALU.subtract))
                s_r1 = S(lambda b: b.activation(rowf[0:1, 512:1024], rowf[0:1, 512:1024],
                                                AF.Ln, bias=consts[0:1, CEPS:CEPS + 1]), wait=v_r4)
                s_r2 = S(lambda b: b.activation(rowf[0:1, 512:1024], rowf[0:1, 512:1024],
                                                AF.Exp, scale=consts[0:1, CNH:CNH + 1]))
                v_rb = V(lambda b: b.tensor_copy(rowsb[0:1, 0:512], rowf[0:1, 0:512]), wait=s_r2)
                v_rb = V(lambda b: b.tensor_copy(rowsb[0:1, 512:1024], rowf[0:1, 512:1024]))
                t_b1 = T(lambda b: b.matmul(psB[:, 1024:1536], ones1r[:, :], rowsb[:, 0:512],
                                            start=True, stop=True), wait=v_rb)
                v_mb = V(lambda b: b.tensor_copy(mbt[:, :], psB[:, 1024:1536]), wait=t_b1)
                t_b2 = T(lambda b: b.matmul(psB[:, 1024:1536], ones1r[:, :], rowsb[:, 512:1024],
                                            start=True, stop=True), wait=v_mb)
                v_rsb = V(lambda b: b.tensor_copy(rsbt[:, :], psB[:, 1024:1536]), wait=t_b2)
                tr_free = v_rsb
                pB_free = v_rsb
                yg_done = None
                for c in range(16):
                    V(lambda b, cc=c: b.tensor_tensor(
                        ygs[:, :], aB[:, 512 * cc:512 * cc + 512], mbt[:, :], ALU.subtract),
                        wait=[v_mb, v_rsb, gate_ev[15]] + dbgw() if c == 0 else None)
                    V(lambda b, cc=c: b.tensor_tensor(ygs[:, :], ygs[:, :], rsbt[:, :], ALU.mult))
                    V(lambda b, cc=c: b.tensor_scalar(
                        ygs[:, :], ygs[:, :], lngc[:, cc:cc + 1], lnbc[:, cc:cc + 1],
                        ALU.mult, ALU.add))
                    yg_done = V(lambda b, cc=c: b.tensor_tensor(
                        xnf[:, 512 * cc:512 * cc + 512], ygs[:, :],
                        aA[:, 8192 + 512 * cc:8192 + 512 * cc + 512], ALU.mult),
                        wait=[gate_ev[c], xnf_ready if c == 0 else None])
                dbg_store(f"yg{p}", xnf[:, :], yg_done)
                xnf_free = None

                s_ev = None
                for n in range(8):
                    i6 = ring_next()
                    sl6, rd6 = ring.get(i6)
                    for m in range(NM):
                        wob = bank((n + m) % 2)
                        tmm = None
                        for dc in range(16):
                            tmm = T(lambda b, s_=sl6, n_=n, mm=m, d_=dc, bk=wob: b.matmul(
                                bk[:, 0:256],
                                xnf[:, 512 * d_ + 128 * mm:512 * d_ + 128 * mm + 128],
                                s_[:, 256 * d_:256 * d_ + 256],
                                start=(d_ == 0), stop=(d_ == 15)),
                                wait=([rd6, yg_done if (n == 0 and m == 0) else None,
                                       bank_free[(n + m) % 2]]
                                      + (dbgw() if (n == 0 and m == 0) else []))
                                if dc == 0 else None)
                        s_ev = V(lambda b, n_=n, mm=m, bk=wob: b.tensor_copy(
                            aB[:, 8192 + 2048 * mm + 256 * n_:8192 + 2048 * mm + 256 * n_ + 256],
                            bk[:, 0:256]),
                            wait=[tmm, gate_ev[15] if (n == 0 and m == 0) else None])
                        bank_free[(n + m) % 2] = s_ev
                    ring.done(i6, (tsm, tsm.n))
                wo_done = s_ev
                xnf_free = wo_done
                dbg_store(f"s{p}", aB[:, 8192:16384], wo_done)

                res_done = None
                for m in range(NM):
                    ld = GD(lambda b, mm=m, xb_=xb, r_=r0: b.dma_start(
                        out=xslot(mm), in_=xb_[r_ + 128 * mm:r_ + 128 * mm + 128, :]),
                        GX[m % 2], wait=xp_free[m % 2])
                    gsrc = ktms if m < 2 else kfs
                    vr = None
                    for cch in range(4):
                        V(lambda b, mm=m, cc=cch, gs_=gsrc: b.tensor_tensor(
                            ygs[:, :],
                            aB[:, 8192 + 2048 * mm + 512 * cc:8192 + 2048 * mm + 512 * cc + 512],
                            gs_[:, 2048 * mm + 512 * cc:2048 * mm + 512 * cc + 512], ALU.mult),
                            wait=([ld, wo_done, ffn_done] + dbgw()
                                  if (m == 0 and cch == 0)
                                  else ([ld] if cch == 0 else None)))
                        vr = V(lambda b, mm=m, cc=cch: b.tensor_tensor(
                            xslot(mm)[:, 512 * cc:512 * cc + 512],
                            xslot(mm)[:, 512 * cc:512 * cc + 512], ygs[:, :], ALU.add))
                    sv = ST(lambda b, mm=m, bs=bsel, r_=r0: b.dma_start(
                        out=out_d[bs, r_ + 128 * mm:r_ + 128 * mm + 128, :], in_=xslot(mm)),
                        SO[m % 2], wait=vr)
                    xp_free[m % 2] = sv
                    res_done = sv
                dbg_store(f"gk{p}", ktms[:, :], ffn_done)
                dbg_store(f"fst{p}", st2[:, :], ffn_done)
                dbg_store("wc", wcols[:, 0:4], ffn_done)
                dbg_store(f"k2f{p}", aB[:, :], [ktr])
                dd = dbgw()
                kfs_free = [res_done] + dd
                kt_free = [[res_done] + dd, [res_done] + dd]
                aB_free = [res_done] + dd
                aA_free = [yg_done] + dd
                xnf_free = ([xnf_free] + dd) if not isinstance(xnf_free, list) else (xnf_free + dd)
                pA_free = bank_free[0]

        except _StopBuild:
            pass
        P.emit("sync", None, wait=[(s, s.n) for s in store_sems])

        with nc.Block() as block:
            @block.sync
            def _(b):
                P.replay("sync", b)

            @block.gpsimd
            def _(b):
                P.replay("gpsimd", b)

            @block.vector
            def _(b):
                P.replay("vector", b)

            @block.scalar
            def _(b):
                P.replay("scalar", b)

            @block.tensor
            def _(b):
                P.replay("tensor", b)

    return nc


def _host_consts():
    ident = np.eye(128).astype(ml_dtypes.bfloat16)
    ident16 = np.zeros((128, 16), dtype=np.float32)
    ident16[0:16] = np.eye(16)
    ident16[32:48] = np.eye(16)
    ident16 = ident16.astype(ml_dtypes.bfloat16)
    iotacol = (127.0 - np.arange(128, dtype=np.float32)).reshape(128, 1)
    return ident, ident16, iotacol


def make_in_maps(inputs, n_passes=NPASS, dbg=()):
    x = np.ascontiguousarray(np.asarray(inputs["L0_attn_x"], dtype=np.float32))
    Wq = np.ascontiguousarray(np.asarray(inputs["L0_attn_Wq"], dtype=np.float32))
    Wv = np.ascontiguousarray(np.asarray(inputs["L0_attn_Wv"], dtype=np.float32))
    Wg = np.ascontiguousarray(np.asarray(inputs["L0_attn_Wg"], dtype=np.float32))
    Wo = np.ascontiguousarray(np.asarray(inputs["L0_attn_Wo"], dtype=np.float32))
    beta = np.asarray(inputs["L0_attn_beta"], dtype=np.float32).reshape(16, 1)
    lng = np.asarray(inputs["L0_attn_ln_g"], dtype=np.float32).reshape(1, D)
    lnb = np.asarray(inputs["L0_attn_ln_b"], dtype=np.float32).reshape(1, D)
    w1 = np.ascontiguousarray(np.asarray(inputs["L0_ffn_w1"], dtype=np.float32))
    w2 = np.ascontiguousarray(np.asarray(inputs["L0_ffn_w2"], dtype=np.float32))
    ident, ident16, iotacol = _host_consts()
    in_maps = []
    for k in range(NCORE):
        b0 = k
        b1 = 7 - k
        pref = np.zeros((PREF, D), dtype=np.float32)
        n0 = b0 * SLICE
        n1 = b1 * SLICE
        pref[0:n0] = x[0, 0:n0]
        pref[n0:n0 + n1] = x[1, 0:n1]
        onehot = np.zeros((128, PGRP), dtype=np.float32)
        gsel = 2 * k - 1
        if gsel >= 0:
            onehot[:, gsel] = 1.0
        nb = np.full((128, 1), float(PGRP - 2 * k), dtype=np.float32)
        in_maps.append(dict(
            x0=np.ascontiguousarray(x[0, n0:n0 + SLICE]),
            x1=np.ascontiguousarray(x[1, n1:n1 + SLICE]),
            xpref=pref,
            Wq=Wq, Wv=Wv, Wg=Wg, Wo=Wo, w1=w1, w2=w2,
            beta16=beta, lng_r=lng, lnb_r=lnb,
            ident=ident, ident16=ident16, iotacol=iotacol,
            onehot=onehot, nbound=nb,
        ))
    return in_maps


def kernel(**inputs):
    nc = build()
    in_maps = make_in_maps(inputs)
    res = run_bass_kernel_spmd(nc, in_maps, core_ids=list(range(NCORE)))
    x = np.asarray(inputs["L0_attn_x"], dtype=np.float32)
    out = np.empty((2, 16384, D), dtype=np.float32)
    for k in range(NCORE):
        o = np.asarray(res.results[k]["out"])
        out[0, k * SLICE:(k + 1) * SLICE] = o[0]
        out[1, (7 - k) * SLICE:(8 - k) * SLICE] = o[1]
    return out


# version 04
# speedup vs baseline: 1.0076x; 1.0076x over previous
import numpy as np
import ml_dtypes
import concourse.bass as bass
import concourse.mybir as mybir
from concourse.bass_utils import run_bass_kernel_spmd
from contextlib import ExitStack

F32 = mybir.dt.float32
BF16 = mybir.dt.bfloat16
I32 = mybir.dt.int32
AF = mybir.ActivationFunctionType
ALU = mybir.AluOpType
AX = mybir.AxisListType

D = 2048
INNER = 8192
NCORE = 8
SLICE = 2048
NT = 512
NPASS = 8
NM = 4
PREF = 14336
PGRP = 14
LN_EPS = 1e-5
SQ2048 = 45.254833995939045

ENGS = ("sync", "gpsimd", "vector", "scalar", "tensor")


class _StopBuild(Exception):
    pass


class Sem:
    __slots__ = ("h", "n", "name")

    def __init__(self, h, name):
        self.h, self.n, self.name = h, 0, name


class Prog:
    def __init__(self):
        self.q = {e: [] for e in ENGS}

    def emit(self, eng, fn=None, wait=None, inc=None, inc_n=1):
        if wait is None:
            wait = []
        elif isinstance(wait, tuple):
            wait = [wait]
        flat = []
        for w in wait:
            if w is None:
                continue
            if isinstance(w, list):
                flat.extend(x for x in w if x is not None)
            else:
                flat.append(w)
        wait = flat
        for s, v in wait:
            assert v <= s.n, f"{eng}: wait {s.name}>={v} but only {s.n} emitted"
        if inc is not None:
            inc.n += inc_n
        self.q[eng].append((fn, wait, inc, inc_n))
        return (inc, inc.n) if inc is not None else None

    def replay(self, eng, b):
        for fn, waits, inc, inc_n in self.q[eng]:
            for s, v in waits:
                b.wait_ge(s.h, v)
            inst = fn(b) if fn is not None else None
            if inc is not None:
                inst.then_inc(inc.h, inc_n)


def build(n_passes=NPASS, dbg=(), stop_after=None):
    nc = bass.Bass("TRN2", target_bir_lowering=False)

    def dt(name, shape, dtyp, kind):
        return nc.dram_tensor(name, shape, dtyp, kind=kind).ap()

    x0 = dt("x0", [SLICE, D], F32, "ExternalInput")
    x1 = dt("x1", [SLICE, D], F32, "ExternalInput")
    xpref = dt("xpref", [PREF, D], F32, "ExternalInput")
    Wq_d = dt("Wq", [D, D], F32, "ExternalInput")
    Wv_d = dt("Wv", [D, D], F32, "ExternalInput")
    Wg_d = dt("Wg", [D, D], F32, "ExternalInput")
    Wo_d = dt("Wo", [D, D], F32, "ExternalInput")
    w1_d = dt("w1", [INNER, D], F32, "ExternalInput")
    w2_d = dt("w2", [D, INNER], F32, "ExternalInput")
    beta16 = dt("beta16", [16, 1], F32, "ExternalInput")
    lng_r = dt("lng_r", [1, D], F32, "ExternalInput")
    lnb_r = dt("lnb_r", [1, D], F32, "ExternalInput")
    ident_i = dt("ident", [128, 128], BF16, "ExternalInput")
    ident16_i = dt("ident16", [128, 16], BF16, "ExternalInput")
    iotacol_i = dt("iotacol", [128, 1], F32, "ExternalInput")
    onehot_i = dt("onehot", [128, PGRP], F32, "ExternalInput")
    nbound_i = dt("nbound", [128, 1], F32, "ExternalInput")

    out_d = dt("out", [2, SLICE, D], F32, "ExternalOutput")
    dbg_outs = {name: dt("dbg_" + name, shape, dty, "ExternalOutput")
                for (name, shape, dty) in dbg}

    sqv_d = dt("sqv", [16, 128, 16, 128], BF16, "Internal")
    sqq_d = dt("sqq", [16, 128, 16, 128], BF16, "Internal")
    sqg_d = dt("sqg", [16, 128, 16, 128], BF16, "Internal")
    sqo_d = dt("sqo", [8, 128, 16, 256], BF16, "Internal")
    wq1t_d = dt("wq1t", [16, 2, 128, 8, 512], BF16, "Internal")
    wq2t_d = dt("wq2t", [64, 128, 2048], BF16, "Internal")
    colscr = dt("colscr", [128, 1], F32, "Internal")
    rowscr = dt("rowscr", [16, 1], F32, "Internal")

    P = Prog()

    with ExitStack() as ctx:
        def sbt(name, shape, dtyp=F32):
            return ctx.enter_context(nc.sbuf_tensor(name, shape, dtyp))

        aA = sbt("aA", [128, 16384], BF16)
        aB = sbt("aB", [128, 16384], BF16)
        kfs = sbt("kfs", [128, 8192], BF16)
        xnf = sbt("xnf", [128, 8192], BF16)
        xping = sbt("xping", [128, 4096], F32)
        xntm = sbt("xntm", [128, 4096], BF16)
        intscr = sbt("intscr", [128, 1024], I32)
        ktms = sbt("ktms", [128, 4096], BF16)
        wr = [sbt(f"wr{i}", [128, 4096], BF16) for i in range(3)]
        prefb = sbt("prefb", [128, 4096], BF16)
        sqd = sbt("sqd", [128, 512], F32)
        ygs = sbt("ygs", [128, 512], BF16)
        ysq = sbt("ysq", [128, 512], BF16)
        mbt = sbt("mbt", [128, 512], BF16)
        rsbt = sbt("rsbt", [128, 512], BF16)
        rowsb = sbt("rowsb", [128, 1024], BF16)
        rowb2 = sbt("rowb2", [128, 2048], BF16)
        rowf = sbt("rowf", [1, 1536], F32)
        stats = sbt("stats", [128, 64], F32)
        st2 = sbt("st2", [128, 32], F32)
        pstat = sbt("pstat", [128, 64], F32)
        wcols = sbt("wcols", [128, 16], F32)
        consts = sbt("consts", [128, 8], F32)
        identsb = sbt("identsb", [128, 128], BF16)
        id16sb = sbt("id16sb", [128, 16], BF16)
        iotac = sbt("iotac", [128, 1], F32)
        ohsb = sbt("ohsb", [128, PGRP], F32)
        nbsb = sbt("nbsb", [128, 1], F32)
        lamcol = sbt("lamcol", [16, 1], F32)
        lnlamcol = sbt("lnlamcol", [16, 1], F32)
        lam1024 = sbt("lam1024", [16, 1], F32)
        pwcol = sbt("pwcol", [16, 1], F32)
        lnlbc = sbt("lnlbc", [128, 16], F32)
        lamfill = sbt("lamfill", [128, 16], F32)
        expbase = sbt("expbase", [128, 16], F32)
        expscr = sbt("expscr", [128, 16], F32)
        BASEt = sbt("BASEt", [128, 128], BF16)
        BASEs = sbt("BASEs", [128, 32], BF16)
        mb8 = sbt("mb8", [128, 8], BF16)
        lngc = sbt("lngc", [128, 16], F32)
        lnbc = sbt("lnbc", [128, 16], F32)
        zrun = sbt("zrun", [16, 2048], F32)
        z0sel = sbt("z0sel", [16, 2048], BF16)
        zb16 = sbt("zb16", [128, 2048], BF16)
        zfm = sbt("zfm", [128, 768], BF16)
        srun = sbt("srun", [16, 8], F32)
        sb16 = sbt("sb16", [128, 2], BF16)
        carry = sbt("carry", [128, 16], F32)
        C0sb = sbt("C0sb", [128, 16], F32)
        C1sb = sbt("C1sb", [128, 16], F32)
        onescol = sbt("onescol", [128, 1], BF16)
        ones1r = sbt("ones1r", [128, 128], BF16)

        psA = ctx.enter_context(nc.psum_tensor("psA", [128, 2048], F32))
        psB = ctx.enter_context(nc.psum_tensor("psB", [128, 2048], F32))

        sl = {}
        for name in ("tsm", "vsm", "ssm", "gx0", "gx1", "gk0", "gk1", "gp0", "gp1",
                     "gm", "wrs0", "wrs1", "wrs2", "sc", "so0", "so1", "sq0", "sq1",
                     "sg0", "sg1", "sdb"):
            sl[name] = Sem(ctx.enter_context(nc.semaphore(name)), name)
        tsm, vsm, ssm = sl["tsm"], sl["vsm"], sl["ssm"]
        GX = [sl["gx0"], sl["gx1"]]
        GK = [sl["gk0"], sl["gk1"]]
        GP = [sl["gp0"], sl["gp1"]]
        GM = sl["gm"]
        WRS = [sl["wrs0"], sl["wrs1"], sl["wrs2"]]
        SC = sl["sc"]
        SO = [sl["so0"], sl["so1"]]
        SQS = [sl["sq0"], sl["sq1"]]
        SG = [sl["sg0"], sl["sg1"]]
        SDB = sl["sdb"]
        store_sems = [sl["so0"], sl["so1"], sl["sq0"], sl["sq1"], sl["sg0"], sl["sg1"], sl["sdb"]]

        def _selfw(wait, sem):
            w = [(sem, sem.n)] if sem.n > 0 else []
            if wait is None:
                return w
            if isinstance(wait, tuple):
                return [wait] + w
            return list(wait) + w

        def T(fn, wait=None):
            return P.emit("tensor", fn, wait=_selfw(wait, tsm), inc=tsm)

        def T0(fn, wait=None):
            return P.emit("tensor", fn, wait=_selfw(wait, tsm), inc=None)

        def V(fn, wait=None):
            return P.emit("vector", fn, wait=_selfw(wait, vsm), inc=vsm)

        def S(fn, wait=None):
            return P.emit("scalar", fn, wait=_selfw(wait, ssm), inc=ssm)

        def GD(fn, sem, wait=None):
            return P.emit("gpsimd", fn, wait=wait, inc=sem, inc_n=16)

        def SD(fn, sem, wait=None):
            return P.emit("sync", fn, wait=wait, inc=sem, inc_n=16)

        ST = SD

        def SSD(fn, sem, wait=None):
            return P.emit("scalar", fn, wait=wait, inc=sem, inc_n=16)

        def dbg_store(name, src_ap, wait):
            if name in dbg_outs:
                ST(lambda b, s=src_ap, d=dbg_outs[name]: b.dma_start(out=d, in_=s),
                   SDB, wait=wait)

        def dbgw():
            return [(SDB, SDB.n)] if SDB.n > 0 else []

        def bank(b):
            return (psA if b < 4 else psB)[:, 512 * (b % 4):512 * (b % 4) + 512]

        bank_free = [None] * 8

        class WRing:
            def __init__(self):
                self.ready = [None] * 3
                self.free = [None] * 3
                self.srcs = []
                self.head = 0

            def queue(self, src_ap, width):
                self.srcs.append((src_ap, width))

            def _load(self):
                i = self.head
                j = i % 3
                src, w = self.srcs[i]
                if len(src.shape) == 3:
                    k3 = src.shape[1]
                    self.ready[j] = P.emit(
                        "sync",
                        lambda b, s=wr[j], a=src, ww=w, kk=k3: b.dma_start(
                            out=s[:, 0:ww].rearrange("p (k j) -> p k j", k=kk), in_=a),
                        wait=self.free[j], inc=WRS[j], inc_n=16)
                else:
                    self.ready[j] = P.emit(
                        "sync",
                        lambda b, s=wr[j], a=src, ww=w: b.dma_start(out=s[:, 0:ww], in_=a),
                        wait=self.free[j], inc=WRS[j], inc_n=16)
                self.head += 1

            def get(self, i, ahead=1):
                while self.head <= min(i + ahead, len(self.srcs) - 1):
                    self._load()
                return wr[i % 3], self.ready[i % 3]

            def done(self, i, semval):
                self.free[i % 3] = semval

        ring = WRing()
        ring_i = [0]

        def ring_next():
            i = ring_i[0]
            ring_i[0] += 1
            return i

        try:

            CEPS, CNH, CPH = 0, 1, 2
            V(lambda b: b.memset(consts[:, CEPS:CEPS + 1], LN_EPS))
            V(lambda b: b.memset(consts[:, CNH:CNH + 1], -0.5))
            V(lambda b: b.memset(consts[:, CPH:CPH + 1], 0.5))
            v_ones = V(lambda b: b.memset(onescol[:, :], 1.0))
            V(lambda b: b.memset(ones1r[:, :], 0.0))
            V(lambda b: b.memset(ones1r[0:1, :], 1.0))
            V(lambda b: b.memset(zb16[:, :], 0.0))
            V(lambda b: b.memset(sb16[:, :], 0.0))
            V(lambda b: b.memset(rowsb[:, :], 0.0))
            V(lambda b: b.memset(rowb2[:, :], 0.0))
            V(lambda b: b.memset(zrun[:, :], 0.0))
            V(lambda b: b.memset(z0sel[:, :], 0.0))
            v_const = V(lambda b: b.memset(srun[:, :], 0.0))

            SD(lambda b: b.dma_start(out=identsb[:, :], in_=ident_i), SC)
            SD(lambda b: b.dma_start(out=id16sb[:, :], in_=ident16_i), SC)
            SD(lambda b: b.dma_start(out=iotac[:, :], in_=iotacol_i), SC)
            SD(lambda b: b.dma_start(out=ohsb[:, :], in_=onehot_i), SC)
            SD(lambda b: b.dma_start(out=nbsb[:, :], in_=nbound_i), SC)
            SD(lambda b: b.dma_start(out=lamcol[:, :], in_=beta16), SC)
            with nc.allow_non_contiguous_dma("one-time 8KB param layout"):
                SD(lambda b: b.dma_start(out=lngc[:, :],
                                         in_=lng_r.rearrange("one (c p) -> (one p) c", p=128),
                                         allow_slow_non_contiguous=True), SC)
                SD(lambda b: b.dma_start(out=lnbc[:, :],
                                         in_=lnb_r.rearrange("one (c p) -> (one p) c", p=128),
                                         allow_slow_non_contiguous=True), SC)
            ld_consts = (SC, SC.n)

            if stop_after == 'consts':
                raise _StopBuild
            s_lam = S(lambda b: b.activation(lamcol[:, :], lamcol[:, :], AF.Sigmoid), wait=ld_consts)
            s_lnl = S(lambda b: b.activation(lnlamcol[:, :], lamcol[:, :], AF.Ln), wait=s_lam)
            v_t0 = V(lambda b: b.tensor_scalar(srun[:, 3:4], lnlamcol[:, :], 1024.0, None, ALU.mult),
                     wait=[s_lnl, v_const])
            s_1024 = S(lambda b: b.activation(lam1024[:, :], srun[:, 3:4], AF.Exp), wait=v_t0)
            v_t1 = V(lambda b: b.tensor_tensor(srun[:, 4:5], srun[:, 3:4], nbsb[0:16, 0:1], ALU.mult),
                     wait=v_t0)
            s_pw = S(lambda b: b.activation(pwcol[:, :], srun[:, 4:5], AF.Exp), wait=v_t1)
            g_r0 = GD(lambda b: b.dma_start(out=rowscr, in_=lnlamcol[:, :]), GM, wait=s_lnl)
            g_bc = GD(lambda b: b.dma_start(
                out=lnlbc[:, :],
                in_=rowscr.rearrange("p one -> one p").broadcast_to([128, 16])), GM, wait=g_r0)
            s_lf = S(lambda b: b.activation(lamfill[:, :], lnlbc[:, :], AF.Exp), wait=g_bc)
            v_eb = V(lambda b: b.tensor_scalar(expbase[:, :], lnlbc[:, :], iotac[:, 0:1], None, ALU.mult),
                     wait=[g_bc, ld_consts])
            base_all = None
            for j in range(8):
                off = float(128 * (7 - j))
                v_ej = V(lambda b, o=off: b.scalar_tensor_tensor(
                    expscr[:, :], lnlbc[:, :], o, expbase[:, :], ALU.mult, ALU.add),
                    wait=[v_eb, base_all])
                base_all = S(lambda b, jj=j: b.activation(
                    BASEt[:, 16 * jj:16 * jj + 16], expscr[:, :], AF.Exp), wait=v_ej)

            if stop_after == 'params':
                raise _StopBuild
            def xslot(i):
                return xping[:, 2048 * (i % 2):2048 * (i % 2) + 2048]

            def kslot(i):
                return ktms[:, 2048 * (i % 2):2048 * (i % 2) + 2048]

            def stg(i):
                return (xnf if i % 2 == 0 else kfs)[:, 0:2048]

            xslot_free = [None, None]
            kslot_free = [None, None]
            stg_free = [None, None]

            ksi = [0]
            sq_last = None
            colsum_ready = None
            for wsel, (W_src, Wdst, cmajor) in enumerate(
                    ((Wv_d, sqv_d, True), (Wq_d, sqq_d, True),
                     (Wg_d, sqg_d, True), (Wo_d, sqo_d, False))):
                cs_mm = None
                for t in range(16):
                    i = ksi[0]
                    ksi[0] += 1
                    ld = GD(lambda b, s=W_src, tt=t, ii=i: b.dma_start(
                        out=kslot(ii), in_=s[128 * tt:128 * tt + 128, :]),
                        GK[i % 2], wait=kslot_free[i % 2])
                    if cmajor:
                        sv = ST(lambda b, d=Wdst, tt=t, ii=i: b.dma_start(
                            out=d[:, :, tt, :].rearrange("c p j -> p c j"),
                            in_=kslot(ii).rearrange("p (c j) -> p c j", c=16)),
                            SQS[i % 2], wait=ld)
                    else:
                        sv = ST(lambda b, d=Wdst, tt=t, ii=i: b.dma_start(
                            out=d[:, :, tt, :].rearrange("n p j -> p n j"),
                            in_=kslot(ii).rearrange("p (n j) -> p n j", n=8)),
                            SQS[i % 2], wait=ld)
                    if wsel == 0:
                        for nn in range(4):
                            cs_mm = T(lambda b, ii=i, n_=nn: b.matmul(
                                psA[0:1, 512 * n_:512 * n_ + 512], onescol[:, :],
                                kslot(ii)[:, 512 * n_:512 * n_ + 512],
                                start=(ii == 0), stop=(ii == 15)),
                                wait=[ld, v_ones] if nn == 0 else None)
                        sv2 = V(lambda b: b.tensor_copy(rowb2[0:1, 0:2048], psA[0:1, 0:2048]),
                                wait=cs_mm) if t == 15 else None
                        if t == 15:
                            colsum_ready = sv2
                        kslot_free[i % 2] = [sv, cs_mm] if cs_mm else sv
                    else:
                        kslot_free[i % 2] = sv
                    sq_last = sv

            if stop_after == 'sq':
                raise _StopBuild
            SW1, E1B, SW2, E2B, WT0, WT1, WP0 = 0, 1, 2, 3, 4, 5, 8
            xai = [0]
            mean_deps = []
            for wi in range(2):
                last_abs = None
                for t in range(64):
                    i = xai[0]
                    xai[0] += 1
                    if wi == 0:
                        src = w1_d[128 * t:128 * t + 128, :]
                    else:
                        src = w2_d[128 * (t // 4):128 * (t // 4) + 128,
                                   2048 * (t % 4):2048 * (t % 4) + 2048]
                    ld = GD(lambda b, s=src, ii=i: b.dma_start(out=xslot(ii), in_=s),
                            GX[i % 2], wait=xslot_free[i % 2])
                    sv = None
                    for cch in range(4):
                        sv = S(lambda b, ii=i, cc=cch: b.activation(
                            sqd[:, :], xslot(ii)[:, 512 * cc:512 * cc + 512], AF.Abs,
                            accum_out=wcols[:, WP0 + cc:WP0 + cc + 1]),
                            wait=[ld, last_abs] if cch == 0 else None)
                    va = V(lambda b, tt=t: b.reduce_sum(
                        pstat[:, tt:tt + 1], wcols[:, WP0:WP0 + 4], AX.X), wait=sv)
                    xslot_free[i % 2] = va
                    last_abs = va
                vr = V(lambda b: b.reduce_sum(wcols[:, WT0:WT0 + 1], pstat[:, 0:64], AX.X),
                       wait=last_abs)
                g1 = GD(lambda b: b.dma_start(out=colscr, in_=wcols[:, WT0:WT0 + 1]), GM, wait=vr)
                g2 = GD(lambda b: b.dma_start(out=rowf[0:1, 0:128],
                                              in_=colscr.rearrange("p one -> one p")), GM, wait=g1)
                vr2 = V(lambda b: b.reduce_sum(rowf[0:1, 128:129], rowf[0:1, 0:128], AX.X), wait=g2)
                g3 = GD(lambda b: b.dma_start(out=colscr[0:1, :], in_=rowf[0:1, 128:129]), GM, wait=vr2)
                g4 = GD(lambda b: b.dma_start(out=wcols[:, WT1:WT1 + 1],
                                              in_=colscr[0:1, :].broadcast_to([128, 1])), GM, wait=g3)
                SWc, EBc = (SW1, E1B) if wi == 0 else (SW2, E2B)
                V(lambda b: b.tensor_scalar(wcols[:, WT1:WT1 + 1], wcols[:, WT1:WT1 + 1],
                                            1.0 / (INNER * D), 1e-5, ALU.mult, ALU.max), wait=g4)
                V(lambda b, sc=SWc: b.reciprocal(wcols[:, sc:sc + 1], wcols[:, WT1:WT1 + 1]))
                md = V(lambda b, eb=EBc: b.tensor_scalar(wcols[:, eb:eb + 1], wcols[:, WT1:WT1 + 1],
                                                         1.0, None, ALU.mult))
                mean_deps.append(md)

            if stop_after == 'absmean':
                raise _StopBuild
            int_free = [None]
            prep_t_free = None

            def prep_quant(wi):
                nonlocal prep_t_free
                SWc = SW1 if wi == 0 else SW2
                tdone = None
                for t in range(64):
                    i = xai[0]
                    xai[0] += 1
                    if wi == 0:
                        src = w1_d[128 * t:128 * t + 128, :]
                    else:
                        src = w2_d[128 * (t // 4):128 * (t // 4) + 128,
                                   2048 * (t % 4):2048 * (t % 4) + 2048]
                    ld = GD(lambda b, s=src, ii=i: b.dma_start(out=xslot(ii), in_=s),
                            GX[i % 2], wait=[xslot_free[i % 2], mean_deps[wi] if t < 2 else None])
                    q2 = None
                    for qc in range(2):
                        q1 = V(lambda b, ii=i, q_=qc: b.tensor_scalar(
                            intscr[:, :], xslot(ii)[:, 1024 * q_:1024 * q_ + 1024],
                            wcols[:, SWc:SWc + 1], None, ALU.mult),
                            wait=[ld, int_free[0]] if qc == 0 else None)
                        q2 = V(lambda b, ii=i, q_=qc: b.tensor_scalar(
                            kslot(ii)[:, 1024 * q_:1024 * q_ + 1024], intscr[:, :],
                            1.0, -1.0, ALU.min, ALU.max),
                            wait=kslot_free[i % 2] if qc == 0 else None)
                    xslot_free[i % 2] = q2
                    int_free[0] = q2
                    tmm = None
                    for c in range(16):
                        tmm = T(lambda b, ii=i, cc=c: b.matmul(
                            psB[:, 128 * cc:128 * cc + 128],
                            kslot(ii)[:, 128 * cc:128 * cc + 128], identsb[:, :],
                            start=True, stop=True),
                            wait=[q2, prep_t_free] if c == 0 else None)
                    ev = V(lambda b, ii=i: b.tensor_copy(stg(ii)[:, :], psB[:, 0:2048]),
                           wait=[tmm, stg_free[i % 2]])
                    prep_t_free = ev
                    kslot_free[i % 2] = ev
                    if wi == 0:
                        for ih in range(2):
                            sv = ST(lambda b, tt=t, ii=i, hh=ih: b.dma_start(
                                out=wq1t_d[tt // 4, hh, :, :, 128 * (tt % 4):128 * (tt % 4) + 128]
                                    .rearrange("p ic j -> p ic j"),
                                in_=stg(ii)[:, 1024 * hh:1024 * hh + 1024]
                                    .rearrange("p (ic j) -> p ic j", ic=8)),
                                SG[i % 2], wait=ev if ih == 0 else None)
                    else:
                        qr, tb = t % 4, t // 4
                        sv = ST(lambda b, q=qr, tbb=tb, ii=i: b.dma_start(
                            out=wq2t_d[16 * q:16 * q + 16, :, 128 * tbb:128 * tbb + 128]
                                .rearrange("kc p j -> p kc j"),
                            in_=stg(ii).rearrange("p (kc j) -> p kc j", kc=16)),
                            SG[i % 2], wait=ev)
                    stg_free[i % 2] = sv
                    tdone = sv
                return tdone

            w1_done = prep_quant(0)
            w2_done = prep_quant(1)

            if stop_after == 'quant':
                raise _StopBuild
            PSX, PXP, PSXX, PMEAN, PVAR, PRS, PTMP = 0, 8, 12, 20, 28, 36, 44
            zps_free = None
            sps_free = None
            prefb_free = [None, None]
            bases_free = [None, None]
            t_zmm = t_smm = None
            vz0 = vs0 = None
            prev_v2 = None
            for g in range(PGRP):
                for jp in range(4):
                    pr = []
                    for jj in range(2):
                        j = 2 * jp + jj
                        i = 8 * g + j
                        ld = GD(lambda b, ii=i: b.dma_start(
                            out=prefb[:, 2048 * (ii % 2):2048 * (ii % 2) + 2048],
                            in_=xpref[128 * ii:128 * ii + 128, :]), GP[i % 2],
                            wait=prefb_free[i % 2])
                        pslot = prefb[:, 2048 * (i % 2):2048 * (i % 2) + 2048]
                        v1 = V(lambda b, ps=pslot, j_=j: b.reduce_sum(
                            pstat[:, PSX + j_:PSX + j_ + 1], ps, AX.X), wait=ld)
                        sv = None
                        for cch in range(4):
                            sv = S(lambda b, ps=pslot, cc=cch: b.activation(
                                sqd[:, :], ps[:, 512 * cc:512 * cc + 512], AF.Square,
                                accum_out=pstat[:, PXP + cc:PXP + cc + 1]),
                                wait=[ld, prev_v2,
                                      w2_done if (g == 0 and j == 0 and cch == 0) else None]
                                if cch == 0 else None)
                        v2 = V(lambda b, j_=j: b.reduce_sum(
                            pstat[:, PSXX + j_:PSXX + j_ + 1], pstat[:, PXP:PXP + 4], AX.X), wait=sv)
                        prev_v2 = v2
                        pr.append((v1, v2))
                    j0 = 2 * jp
                    V(lambda b, s0=j0: b.tensor_scalar(
                        pstat[:, PMEAN + s0:PMEAN + s0 + 2], pstat[:, PSX + s0:PSX + s0 + 2],
                        1.0 / D, None, ALU.mult), wait=[pr[0][0], pr[0][1], pr[1][0], pr[1][1]])
                    V(lambda b, s0=j0: b.tensor_scalar(
                        pstat[:, PTMP + s0:PTMP + s0 + 2], pstat[:, PSXX + s0:PSXX + s0 + 2],
                        1.0 / D, None, ALU.mult))
                    V(lambda b, s0=j0: b.tensor_tensor(
                        pstat[:, PVAR + s0:PVAR + s0 + 2], pstat[:, PMEAN + s0:PMEAN + s0 + 2],
                        pstat[:, PMEAN + s0:PMEAN + s0 + 2], ALU.mult))
                    vv = V(lambda b, s0=j0: b.tensor_tensor(
                        pstat[:, PVAR + s0:PVAR + s0 + 2], pstat[:, PTMP + s0:PTMP + s0 + 2],
                        pstat[:, PVAR + s0:PVAR + s0 + 2], ALU.subtract))
                    sv1 = S(lambda b, s0=j0: b.activation(
                        pstat[:, PTMP + s0:PTMP + s0 + 2], pstat[:, PVAR + s0:PVAR + s0 + 2],
                        AF.Ln, bias=consts[:, CEPS:CEPS + 1]), wait=vv)
                    sv2 = S(lambda b, s0=j0: b.activation(
                        pstat[:, PRS + s0:PRS + s0 + 2], pstat[:, PTMP + s0:PTMP + s0 + 2],
                        AF.Exp, scale=consts[:, CNH:CNH + 1]))
                    vmb = V(lambda b, s0=j0: b.tensor_copy(
                        mb8[:, s0:s0 + 2], pstat[:, PMEAN + s0:PMEAN + s0 + 2]), wait=sv2)
                    for jj in range(2):
                        j = 2 * jp + jj
                        i = 8 * g + j
                        vbs = V(lambda b, j_=j, ii=i: b.tensor_scalar(
                            BASEs[:, 16 * (ii % 2):16 * (ii % 2) + 16],
                            BASEt[:, 16 * j_:16 * j_ + 16],
                            pstat[:, PRS + j_:PRS + j_ + 1], None, ALU.mult),
                            wait=[sv2, bases_free[i % 2],
                                  base_all if (g == 0 and j < 2) else None])
                        pslot = prefb[:, 2048 * (i % 2):2048 * (i % 2) + 2048]
                        for nn in range(4):
                            t_zmm = T(lambda b, ii=i, n_=nn, j_=j: b.matmul(
                                psB[0:16, 512 * n_:512 * n_ + 512],
                                BASEs[:, 16 * (ii % 2):16 * (ii % 2) + 16],
                                prefb[:, 2048 * (ii % 2) + 512 * n_:2048 * (ii % 2) + 512 * n_ + 512],
                                start=(j_ == 0), stop=(j_ == 7)),
                                wait=([vbs, zps_free, prep_t_free if g == 0 else None]
                                      if (j == 0 and nn == 0) else (vbs if nn == 0 else None)))
                        t_smm = T(lambda b, ii=i, j_=j: b.matmul(
                            psA[0:16, 512:513],
                            BASEs[:, 16 * (ii % 2):16 * (ii % 2) + 16],
                            mb8[:, j_:j_ + 1],
                            start=(j_ == 0), stop=(j_ == 7)),
                            wait=[vmb, sps_free if j == 0 else None,
                                  colsum_ready if (g == 0 and j == 0) else None])
                        bases_free[i % 2] = t_smm
                        prefb_free[i % 2] = t_smm
                vch = V(lambda b: b.scalar_tensor_tensor(
                    zrun[:, :], zrun[:, :], lam1024[:, 0:1], psB[0:16, 0:2048], ALU.mult, ALU.add),
                    wait=[t_zmm, s_1024])
                zps_free = vch
                vz0 = V(lambda b, gg=g: b.scalar_tensor_tensor(
                    z0sel[:, :], zrun[:, :], ohsb[0:16, gg:gg + 1], z0sel[:, :], ALU.mult, ALU.add))
                vsch = V(lambda b: b.scalar_tensor_tensor(
                    srun[:, 0:1], srun[:, 0:1], lam1024[:, 0:1], psA[0:16, 512:513],
                    ALU.mult, ALU.add), wait=t_smm)
                sps_free = vsch
                vs0 = V(lambda b, gg=g: b.scalar_tensor_tensor(
                    srun[:, 1:2], srun[:, 0:1], ohsb[0:16, gg:gg + 1], srun[:, 1:2],
                    ALU.mult, ALU.add))

            if stop_after == 'chain':
                raise _StopBuild
            v_zc = V(lambda b: b.tensor_copy(zb16[0:16, :], z0sel[:, :]), wait=[vz0, zps_free])
            if stop_after == "zc":
                raise _StopBuild
            tzl = []
            for c in range(16):
                em = T if c == 15 else T0
                tzl.append(em(lambda b, cc=c: b.matmul(
                    psB[:, 16 * cc:16 * cc + 16], zb16[:, 128 * cc:128 * cc + 128],
                    id16sb[:, :], start=True, stop=True), wait=v_zc if c == 0 else None))
            V(lambda b: b.memset(zfm[:, :], 0.0), wait=None)
            v_zf = None
            for c in range(16):
                v_zf = V(lambda b, cc=c: b.tensor_copy(
                    zfm[:, 48 * cc:48 * cc + 16], psB[:, 16 * cc:16 * cc + 16]),
                    wait=tzl[-1] if c == 0 else None)
            if stop_after == "zf1":
                raise _StopBuild
            v_z1 = V(lambda b: b.tensor_scalar(z0sel[:, :], z0sel[:, :], pwcol[:, 0:1], None, ALU.mult),
                     wait=[v_zf, s_pw])
            v_z1 = V(lambda b: b.tensor_tensor(z0sel[:, :], zrun[:, :], z0sel[:, :], ALU.subtract))
            v_zc1 = V(lambda b: b.tensor_copy(zb16[0:16, :], z0sel[:, :]))
            tzl = []
            for c in range(16):
                em = T if c == 15 else T0
                tzl.append(em(lambda b, cc=c: b.matmul(
                    psB[:, 16 * cc:16 * cc + 16], zb16[:, 128 * cc:128 * cc + 128],
                    id16sb[:, :], start=True, stop=True), wait=v_zc1 if c == 0 else None))
            for c in range(16):
                v_zf = V(lambda b, cc=c: b.tensor_copy(
                    zfm[:, 48 * cc + 32:48 * cc + 48], psB[:, 16 * cc:16 * cc + 16]),
                    wait=tzl[-1] if c == 0 else None)
            zfm_done = v_zf
            if stop_after == 'zfm':
                raise _StopBuild
            v_s1 = V(lambda b: b.tensor_scalar(srun[:, 2:3], srun[:, 1:2], pwcol[:, 0:1], None, ALU.mult),
                     wait=[vs0, s_pw, sps_free])
            v_s1 = V(lambda b: b.tensor_tensor(srun[:, 2:3], srun[:, 0:1], srun[:, 2:3], ALU.subtract))
            V(lambda b: b.tensor_scalar(sb16[0:16, 0:1], srun[:, 1:2], -1.0, None, ALU.mult))
            v_sn = V(lambda b: b.tensor_scalar(sb16[0:16, 1:2], srun[:, 2:3], -1.0, None, ALU.mult))
            t_sr = T0(lambda b: b.matmul(psB[0:1, 1024:1040], sb16[:, 0:1], id16sb[:, :],
                                         start=True, stop=True), wait=v_sn)
            t_sr = T(lambda b: b.matmul(psB[0:1, 1040:1056], sb16[:, 1:2], id16sb[:, :],
                                        start=True, stop=True))
            v_srow = V(lambda b: b.memset(rowsb[0:1, 0:48], 0.0))
            v_srow = V(lambda b: b.tensor_copy(rowsb[0:1, 0:16], psB[0:1, 1024:1040]), wait=t_sr)
            v_srow = V(lambda b: b.tensor_copy(rowsb[0:1, 32:48], psB[0:1, 1040:1056]))

            if stop_after == 'snegs':
                raise _StopBuild
            wvp_free = [prefb_free[0], prefb_free[1]]
            g_sw = None
            cs_mm = None
            for c in range(16):
                ld = GD(lambda b, cc=c, ii=c: b.dma_start(
                    out=prefb[:, 2048 * (ii % 2):2048 * (ii % 2) + 2048],
                    in_=Wv_d[128 * cc:128 * cc + 128, :]), GP[c % 2], wait=wvp_free[c % 2])
                for nn in range(4):
                    g_sw = T(lambda b, cc=c, n_=nn: b.matmul(
                        psA[0:48, 512 * n_:512 * n_ + 512],
                        zfm[:, 48 * cc:48 * cc + 48],
                        prefb[:, 2048 * (cc % 2) + 512 * n_:2048 * (cc % 2) + 512 * n_ + 512],
                        start=(cc == 0), stop=False),
                        wait=([ld, zfm_done, sps_free] if (c == 0 and nn == 0)
                              else (ld if nn == 0 else None)))
                wvp_free[c % 2] = g_sw
            if stop_after == 'gsweep':
                raise _StopBuild
            v_cs = colsum_ready
            rk = None
            for nn in range(4):
                rk = T(lambda b, n_=nn: b.matmul(
                    psA[0:48, 512 * n_:512 * n_ + 512], rowsb[:, 0:48],
                    rowb2[:, 512 * n_:512 * n_ + 512], start=False, stop=True),
                    wait=[v_cs, v_srow, g_sw] if nn == 0 else None)
            if stop_after == 'rank1':
                raise _StopBuild
            v_cx = None
            for bsel in range(2):
                if bsel == 1:
                    V(lambda b: b.memset(zb16[0:16, :], 0.0), wait=v_cx)
                v_gb = V(lambda b, bs=bsel: b.tensor_copy(
                    zb16[32 * bs:32 * bs + 16, :], psA[32 * bs:32 * bs + 16, 0:2048]),
                    wait=rk if bsel == 0 else v_cx)
                txl = []
                for c in range(16):
                    em = T if c == 15 else T0
                    txl.append(em(lambda b, cc=c: b.matmul(
                        psB[:, 16 * cc:16 * cc + 16], zb16[:, 128 * cc:128 * cc + 128],
                        id16sb[:, :], start=True, stop=True),
                        wait=v_gb if c == 0 else None))
                Cdst = C0sb if bsel == 0 else C1sb
                v_cx = V(lambda b, d=Cdst: b.tensor_copy(d[:, :], psB[:, 0:272:17]), wait=txl[-1])
            prologue_done = v_cx
            dbg_store("C0", C0sb[:, :], prologue_done)
            dbg_store("C1", C1sb[:, :], prologue_done)
            if stop_after == "prologue":
                raise _StopBuild

            for p in range(n_passes):
                for mp in range(2):
                    for np_ in range(16):
                        for ih in range(2):
                            ring.queue(wq1t_d[np_, ih].rearrange("p ic j -> p (ic j)"), 4096)
                    for u in range(32):
                        ring.queue(wq2t_d[2 * u:2 * u + 2].rearrange("k p j -> p k j"), 4096)
                for W in (sqv_d, sqq_d, sqg_d):
                    for c in range(16):
                        ring.queue(W[c].rearrange("p ic j -> p (ic j)"), 2048)
                for n in range(8):
                    ring.queue(sqo_d[n].rearrange("p dc j -> p (dc j)"), 4096)

            SX, SXXP, SXX, MEAN, VAR, RS, ALPH, MX, ASC, E1C, TM0, TM1, NRM = \
                0, 4, 20, 24, 28, 32, 36, 40, 44, 48, 52, 56, 60
            FSH, FMX, FAS, FE2, P8, FT0, FT1, FT2, FNR = 0, 4, 8, 12, 16, 24, 26, 28, 30

            xp_free = [xslot_free[0], xslot_free[1]]
            xn_free = [None, None]
            kt_free = [kslot_free[0], kslot_free[1]]
            int_f = int_free[0]
            xnf_free = stg_free[0]
            kfs_free = stg_free[1]
            aA_free = None
            aB_free = None
            sqd_free = None
            carry_init = None
            tr_free = prep_t_free

            def xnf_dst(m, grp):
                return xnf.rearrange("p (c mm j) -> p c mm j", c=16, mm=4)[:, 4 * grp:4 * grp + 4, m, :]

            def kfs_dst(m, grp):
                return kfs.rearrange("p (c mm j) -> p c mm j", c=16, mm=4)[:, 4 * grp:4 * grp + 4, m, :]

            pA_free = rk
            pB_free = prologue_done

            for p in range(n_passes):
                bsel = p // 4
                xb = x0 if bsel == 0 else x1
                r0 = (p % 4) * NT

                v_apply = [None] * NM
                ph1_ev = None
                for m in range(NM):
                    ld = GD(lambda b, mm=m, xb_=xb, r_=r0: b.dma_start(
                        out=xslot(mm), in_=xb_[r_ + 128 * mm:r_ + 128 * mm + 128, :]),
                        GX[m % 2],
                        wait=[xp_free[m % 2], prologue_done if (p == 0 and m == 0) else None])
                    xs = xslot(m)
                    v1 = V(lambda b, x_=xs, mm=m: b.reduce_sum(
                        stats[:, SX + mm:SX + mm + 1], x_, AX.X), wait=ld)
                    sv = None
                    for cch in range(4):
                        sv = S(lambda b, x_=xs, cc=cch, mm=m: b.activation(
                            sqd[:, :], x_[:, 512 * cc:512 * cc + 512], AF.Square,
                            accum_out=stats[:, SXXP + 4 * mm + cc:SXXP + 4 * mm + cc + 1]),
                            wait=[ld, sqd_free] if cch == 0 else None)
                    sqd_free = sv
                    v2 = V(lambda b, mm=m: b.reduce_sum(
                        stats[:, SXX + mm:SXX + mm + 1],
                        stats[:, SXXP + 4 * mm:SXXP + 4 * mm + 4], AX.X), wait=sv)
                    V(lambda b, mm=m: b.tensor_scalar(
                        stats[:, MEAN + mm:MEAN + mm + 1], stats[:, SX + mm:SX + mm + 1],
                        1.0 / D, None, ALU.mult), wait=[v1, v2])
                    V(lambda b, mm=m: b.tensor_scalar(
                        stats[:, TM0 + mm:TM0 + mm + 1], stats[:, SXX + mm:SXX + mm + 1],
                        1.0 / D, None, ALU.mult))
                    V(lambda b, mm=m: b.tensor_tensor(
                        stats[:, VAR + mm:VAR + mm + 1], stats[:, MEAN + mm:MEAN + mm + 1],
                        stats[:, MEAN + mm:MEAN + mm + 1], ALU.mult))
                    vv = V(lambda b, mm=m: b.tensor_tensor(
                        stats[:, VAR + mm:VAR + mm + 1], stats[:, TM0 + mm:TM0 + mm + 1],
                        stats[:, VAR + mm:VAR + mm + 1], ALU.subtract))
                    s1 = S(lambda b, mm=m: b.activation(
                        stats[:, TM0 + mm:TM0 + mm + 1], stats[:, VAR + mm:VAR + mm + 1],
                        AF.Ln, bias=consts[:, CEPS:CEPS + 1]), wait=vv)
                    s2 = S(lambda b, mm=m: b.activation(
                        stats[:, RS + mm:RS + mm + 1], stats[:, TM0 + mm:TM0 + mm + 1],
                        AF.Exp, scale=consts[:, CNH:CNH + 1]))
                    V(lambda b, mm=m: b.tensor_tensor(
                        stats[:, TM1 + mm:TM1 + mm + 1], stats[:, RS + mm:RS + mm + 1],
                        stats[:, RS + mm:RS + mm + 1], ALU.mult), wait=s2)
                    vv = V(lambda b, mm=m: b.scalar_tensor_tensor(
                        stats[:, NRM + mm:NRM + mm + 1], stats[:, VAR + mm:VAR + mm + 1],
                        float(D), stats[:, TM1 + mm:TM1 + mm + 1], ALU.mult, ALU.mult))
                    s3 = S(lambda b, mm=m: b.activation(
                        stats[:, NRM + mm:NRM + mm + 1], stats[:, NRM + mm:NRM + mm + 1],
                        AF.Ln), wait=vv)
                    s4 = S(lambda b, mm=m: b.activation(
                        stats[:, NRM + mm:NRM + mm + 1], stats[:, NRM + mm:NRM + mm + 1],
                        AF.Exp, scale=consts[:, CPH:CPH + 1]))
                    V(lambda b, mm=m: b.tensor_scalar(
                        stats[:, NRM + mm:NRM + mm + 1], stats[:, NRM + mm:NRM + mm + 1],
                        1e-12, None, ALU.max), wait=s4)
                    V(lambda b, mm=m: b.reciprocal(
                        stats[:, TM1 + mm:TM1 + mm + 1], stats[:, NRM + mm:NRM + mm + 1]))
                    va = V(lambda b, mm=m: b.tensor_scalar(
                        stats[:, ALPH + mm:ALPH + mm + 1], stats[:, TM1 + mm:TM1 + mm + 1],
                        SQ2048, None, ALU.mult))
                    v_apply[m] = V(lambda b, x_=xs, mm=m: b.tensor_scalar(
                        xntm[:, 2048 * (mm % 2):2048 * (mm % 2) + 2048], x_,
                        stats[:, MEAN + mm:MEAN + mm + 1], stats[:, RS + mm:RS + mm + 1],
                        ALU.subtract, ALU.mult), wait=[s2, xn_free[m % 2]])
                    xp_free[m % 2] = v_apply[m]
                    xnslot = xntm[:, 2048 * (m % 2):2048 * (m % 2) + 2048]
                    vmx = V(lambda b, x_=xnslot, mm=m: b.reduce_max(
                        stats[:, MX + mm:MX + mm + 1], x_, AX.X, apply_absolute_value=True))
                    V(lambda b, mm=m: b.tensor_tensor(
                        stats[:, TM0 + mm:TM0 + mm + 1], stats[:, ALPH + mm:ALPH + mm + 1],
                        stats[:, MX + mm:MX + mm + 1], ALU.mult), wait=[va, vmx])
                    V(lambda b, mm=m: b.tensor_scalar(
                        stats[:, TM0 + mm:TM0 + mm + 1], stats[:, TM0 + mm:TM0 + mm + 1],
                        1e-5, None, ALU.max))
                    V(lambda b, mm=m: b.reciprocal(
                        stats[:, TM1 + mm:TM1 + mm + 1], stats[:, TM0 + mm:TM0 + mm + 1]))
                    V(lambda b, mm=m: b.scalar_tensor_tensor(
                        stats[:, ASC + mm:ASC + mm + 1], stats[:, ALPH + mm:ALPH + mm + 1],
                        127.0, stats[:, TM1 + mm:TM1 + mm + 1], ALU.mult, ALU.mult))
                    V(lambda b, mm=m: b.scalar_tensor_tensor(
                        stats[:, E1C + mm:E1C + mm + 1], stats[:, TM0 + mm:TM0 + mm + 1],
                        1.0 / 127.0, wcols[:, E1B:E1B + 1], ALU.mult, ALU.mult))
                    vq2 = None
                    for qc in range(2):
                        V(lambda b, x_=xnslot, mm=m, q_=qc: b.tensor_scalar(
                            intscr[:, :], x_[:, 1024 * q_:1024 * q_ + 1024],
                            stats[:, ASC + mm:ASC + mm + 1], None, ALU.mult),
                            wait=int_f if qc == 0 else None)
                        vq2 = V(lambda b, mm=m, q_=qc: b.tensor_copy(
                            kslot(mm)[:, 1024 * q_:1024 * q_ + 1024], intscr[:, :]),
                            wait=kt_free[m % 2] if qc == 0 else None)
                    int_f = vq2
                    for src_kind in range(2):
                        srcslot = xnslot if src_kind == 0 else kslot(m)
                        dep = v_apply[m] if src_kind == 0 else vq2
                        for grp in range(4):
                            tmm = None
                            for c4 in range(4):
                                c = 4 * grp + c4
                                tmm = T(lambda b, s_=srcslot, cc=c, g4=c4: b.matmul(
                                    psB[:, 512 + 128 * g4:512 + 128 * g4 + 128],
                                    s_[:, 128 * cc:128 * cc + 128], identsb[:, :],
                                    start=True, stop=True),
                                    wait=[dep, tr_free] if c4 == 0 else None)
                            if src_kind == 0:
                                ev = V(lambda b, mm=m, g4=grp: b.tensor_copy(
                                    xnf_dst(mm, g4),
                                    psB[:, 512:1024].rearrange("p (c j) -> p c j", c=4)),
                                    wait=[tmm, xnf_free if (m == 0 and grp == 0) else None])
                            else:
                                ev = V(lambda b, mm=m, g4=grp: b.tensor_copy(
                                    kfs_dst(mm, g4),
                                    psB[:, 512:1024].rearrange("p (c j) -> p c j", c=4)),
                                    wait=[tmm, kfs_free if (m == 0 and grp == 0) else None])
                            tr_free = ev
                    kt_free[m % 2] = ev
                    xn_free[m % 2] = ev
                    ph1_ev = ev
                xnf_ready = ph1_ev
                kfs_ready = ph1_ev
                dbg_store(f"xnf{p}", xnf[:, :], xnf_ready)
                dbg_store(f"kfs{p}", kfs[:, :], kfs_ready)
                dbg_store(f"stats{p}", stats[:, :], xnf_ready)

                g_evac = [None] * NM
                for mp in range(2):
                    h_ev = [None, None]
                    for np_ in range(16):
                        i0 = ring_next()
                        sl0, rd0 = ring.get(i0)
                        i1 = ring_next()
                        sl1, rd1 = ring.get(i1)
                        for mr in range(2):
                            m = 2 * mp + mr
                            bslot = bank(2 * mr + (np_ % 2))
                            tmm = None
                            for ih in range(2):
                                slx, rdx = (sl0, rd0) if ih == 0 else (sl1, rd1)
                                for ic8 in range(8):
                                    ic = 8 * ih + ic8
                                    tmm = T(lambda b, s_=slx, mm=m, icc=ic, i8=ic8, bk=bslot: b.matmul(
                                        bk, kfs[:, 512 * icc + 128 * mm:512 * icc + 128 * mm + 128],
                                        s_[:, 512 * i8:512 * i8 + 512],
                                        start=(icc == 0), stop=(icc == 15)),
                                        wait=([rdx, kfs_ready if (np_ == 0 and mr == 0) else None,
                                               bank_free[2 * mr + (np_ % 2)],
                                               pA_free if (np_ < 2 and mp == 0) else None]
                                              if ic8 == 0 else None))
                            hs = S(lambda b, mm=m, mr_=mr, n_=np_, bk=bslot: b.activation(
                                aA[:, 8192 * mr_ + 512 * n_:8192 * mr_ + 512 * n_ + 512],
                                bk, AF.Gelu, scale=stats[:, E1C + mm:E1C + mm + 1]),
                                wait=[tmm, aA_free if (mp == 0 and np_ == 0) else None])
                            bank_free[2 * mr + (np_ % 2)] = hs
                            h_ev[mr] = hs
                        ring.done(i0, (tsm, tsm.n))
                        ring.done(i1, (tsm, tsm.n))
                    if mp == 0:
                        dbg_store(f"h1_{p}", aA[:, :], [h_ev[0], h_ev[1]])
                    ffn2_dep = [None, None]
                    for mr in range(2):
                        m = 2 * mp + mr
                        hrow = aA[:, 8192 * mr:8192 * mr + 8192]
                        sg2 = S(lambda b, h_=hrow: b.activation(h_, h_, AF.Gelu), wait=[h_ev[mr]] + dbgw())
                        vmx2 = V(lambda b, h_=hrow, mm=m: b.reduce_max(
                            st2[:, FMX + mm:FMX + mm + 1], h_, AX.X, apply_absolute_value=True),
                            wait=sg2)
                        sv = None
                        for cch in range(16):
                            dstc = (st2[:, P8 + cch:P8 + cch + 1] if cch < 8
                                    else pstat[:, cch - 8:cch - 8 + 1])
                            sv = S(lambda b, h_=hrow, cc=cch, dc=dstc: b.activation(
                                sqd[:, :], h_[:, 512 * cc:512 * cc + 512],
                                AF.Square, accum_out=dc), wait=[sg2] if cch == 0 else None)
                        vred = V(lambda b, mm=m: b.reduce_sum(
                            st2[:, FT0 + mm % 2:FT0 + mm % 2 + 1], st2[:, P8:P8 + 8], AX.X),
                            wait=sv)
                        V(lambda b, mm=m: b.reduce_sum(
                            st2[:, FT1 + mm % 2:FT1 + mm % 2 + 1], pstat[:, 0:8], AX.X))
                        vred = V(lambda b, mm=m: b.tensor_tensor(
                            st2[:, FSH + mm:FSH + mm + 1], st2[:, FT0 + mm % 2:FT0 + mm % 2 + 1],
                            st2[:, FT1 + mm % 2:FT1 + mm % 2 + 1], ALU.add))
                        if mp == 0 and mr == 0:
                            dbg_store(f"h_{p}", aA[:, 0:8192], sg2)
                        vv = V(lambda b, mm=m: b.tensor_scalar(
                            st2[:, FNR + mm % 2:FNR + mm % 2 + 1], st2[:, FSH + mm:FSH + mm + 1],
                            1e-24, None, ALU.max), wait=[vmx2, vred])
                        s5 = S(lambda b, mm=m: b.activation(
                            st2[:, FNR + mm % 2:FNR + mm % 2 + 1], st2[:, FNR + mm % 2:FNR + mm % 2 + 1],
                            AF.Ln), wait=vv)
                        s6 = S(lambda b, mm=m: b.activation(
                            st2[:, FNR + mm % 2:FNR + mm % 2 + 1], st2[:, FNR + mm % 2:FNR + mm % 2 + 1],
                            AF.Exp, scale=consts[:, CPH:CPH + 1]))
                        V(lambda b, mm=m: b.tensor_scalar(
                            st2[:, FNR + mm % 2:FNR + mm % 2 + 1], st2[:, FNR + mm % 2:FNR + mm % 2 + 1],
                            1e-12, None, ALU.max), wait=s6)
                        V(lambda b, mm=m: b.reciprocal(
                            st2[:, FT0 + mm % 2:FT0 + mm % 2 + 1], st2[:, FNR + mm % 2:FNR + mm % 2 + 1]))
                        va2 = V(lambda b, mm=m: b.tensor_scalar(
                            st2[:, FT0 + mm % 2:FT0 + mm % 2 + 1], st2[:, FT0 + mm % 2:FT0 + mm % 2 + 1],
                            float(np.sqrt(INNER)), None, ALU.mult))
                        V(lambda b, mm=m: b.tensor_tensor(
                            st2[:, FT1 + mm % 2:FT1 + mm % 2 + 1], st2[:, FT0 + mm % 2:FT0 + mm % 2 + 1],
                            st2[:, FMX + mm:FMX + mm + 1], ALU.mult), wait=va2)
                        V(lambda b, mm=m: b.tensor_scalar(
                            st2[:, FT1 + mm % 2:FT1 + mm % 2 + 1], st2[:, FT1 + mm % 2:FT1 + mm % 2 + 1],
                            1e-5, None, ALU.max))
                        V(lambda b, mm=m: b.reciprocal(
                            st2[:, FT2 + mm % 2:FT2 + mm % 2 + 1], st2[:, FT1 + mm % 2:FT1 + mm % 2 + 1]))
                        V(lambda b, mm=m: b.scalar_tensor_tensor(
                            st2[:, FAS + mm:FAS + mm + 1], st2[:, FT0 + mm % 2:FT0 + mm % 2 + 1],
                            127.0, st2[:, FT2 + mm % 2:FT2 + mm % 2 + 1], ALU.mult, ALU.mult))
                        ve2 = V(lambda b, mm=m: b.scalar_tensor_tensor(
                            st2[:, FE2 + mm:FE2 + mm + 1], st2[:, FT1 + mm % 2:FT1 + mm % 2 + 1],
                            1.0 / 127.0, wcols[:, E2B:E2B + 1], ALU.mult, ALU.mult))
                        q_last = None
                        for cch in range(8):
                            V(lambda b, h_=hrow, cc=cch, mm=m: b.tensor_scalar(
                                intscr[:, :], h_[:, 1024 * cc:1024 * cc + 1024],
                                st2[:, FAS + mm:FAS + mm + 1], None, ALU.mult),
                                wait=[ve2, int_f] + dbgw() if cch == 0 else None)
                            q_last = V(lambda b, h_=hrow, cc=cch: b.tensor_copy(
                                h_[:, 1024 * cc:1024 * cc + 1024], intscr[:, :]))
                            int_f = q_last
                        ffn2_dep[mr] = q_last
                    ktr = None
                    for kc2 in range(32):
                        tmm = None
                        for sub in range(4):
                            kc = 2 * kc2 + sub // 2
                            mr = sub % 2
                            tmm = T(lambda b, kcc=kc, mr_=mr, s4=sub: b.matmul(
                                psB[:, 512 + 128 * s4:512 + 128 * s4 + 128],
                                aA[:, 8192 * mr_ + 128 * kcc:8192 * mr_ + 128 * kcc + 128],
                                identsb[:, :], start=True, stop=True),
                                wait=([ffn2_dep[0], ffn2_dep[1], tr_free] if (kc2 == 0 and sub == 0)
                                      else (tr_free if sub == 0 else None)))
                        ktr = V(lambda b, k2=kc2: b.tensor_copy(
                            aB[:, 512 * k2:512 * k2 + 512],
                            psB[:, 512:1024]),
                            wait=[tmm, aB_free if kc2 == 0 else None])
                        tr_free = ktr
                    o_mm = None
                    for u in range(32):
                        i2 = ring_next()
                        sl2, rd2 = ring.get(i2)
                        for kk in range(2):
                            kc = 2 * u + kk
                            for mr in range(2):
                                psX = psA if mr == 0 else psB
                                for n in range(4):
                                    o_mm = T(lambda b, kcc=kc, mr_=mr, n_=n, s_=sl2, k_=kk, px=psX: b.matmul(
                                        px[:, 512 * n_:512 * n_ + 512],
                                        aB[:, 256 * kcc + 128 * mr_:256 * kcc + 128 * mr_ + 128],
                                        s_[:, 2048 * k_ + 512 * n_:2048 * k_ + 512 * n_ + 512],
                                        start=(kcc == 0), stop=(kcc == 63)),
                                        wait=([rd2, ktr,
                                               pA_free if mr == 0 else pB_free,
                                               bank_free[0], bank_free[1], bank_free[2], bank_free[3],
                                               tr_free]
                                              if (u == 0 and kk == 0 and n == 0)
                                              else (rd2 if (kk == 0 and mr == 0 and n == 0)
                                                    else None)))
                        ring.done(i2, (tsm, tsm.n))
                    for mr in range(2):
                        m = 2 * mp + mr
                        psX = psA if mr == 0 else psB
                        gdst = ktms if m < 2 else kfs
                        gcol = 2048 * m if m < 2 else 2048 * m
                        g_evac[m] = S(lambda b, px=psX, gd=gdst, gc=gcol, mm=m: b.activation(
                            gd[:, gc:gc + 2048], px[:, 0:2048],
                            AF.Silu, scale=st2[:, FE2 + mm:FE2 + mm + 1]),
                            wait=[o_mm, kt_free[mr] if m < 2 else None] + dbgw())
                    pA_free = g_evac[2 * mp + 0]
                    pB_free = g_evac[2 * mp + 1]
                    for b4 in range(4):
                        bank_free[b4] = g_evac[2 * mp + 0]
                    aA_free = ktr
                    aB_free = o_mm
                ffn_done = g_evac[3]
                dbg_store(f"g{p}", kfs[:, :], ffn_done)

                if p == 0 or p == 4:
                    Csrc = C0sb if p == 0 else C1sb
                    carry_init = V(lambda b, cs=Csrc: b.tensor_copy(carry[:, :], cs[:, :]),
                                   wait=[prologue_done, carry_init])
                v_ev = [None] * 16
                scan_done = None
                for c in range(16):
                    i3 = ring_next()
                    sl3, rd3 = ring.get(i3)
                    bnk = c % 2
                    tmm = None
                    for kc in range(16):
                        tmm = T(lambda b, s_=sl3, cc=c, k_=kc: b.matmul(
                            bank(cc % 2), s_[:, 128 * k_:128 * k_ + 128],
                            xnf[:, 512 * k_:512 * k_ + 512],
                            start=(k_ == 0), stop=(k_ == 15)),
                            wait=([rd3, xnf_ready if c == 0 else None, bank_free[c % 2],
                                   aA_free if c < 2 else None,
                                   pA_free if c < 2 else None] if kc == 0 else None))
                    ring.done(i3, (tsm, tsm.n))
                    v_ev[c] = V(lambda b, cc=c: b.tensor_copy(
                        aA[:, 512 * cc:512 * cc + 512], bank(cc % 2)), wait=tmm)
                    bank_free[c % 2] = v_ev[c]
                    scan_done = V(lambda b, cc=c: b.tensor_tensor_scan(
                        aA[:, 8192 + 512 * cc:8192 + 512 * cc + 512],
                        lamfill[:, cc:cc + 1].broadcast_to([128, 512]),
                        aA[:, 512 * cc:512 * cc + 512],
                        carry[:, cc:cc + 1], ALU.mult, ALU.add),
                        wait=[carry_init, s_lf] if c == 0 else None)
                v_carry = V(lambda b: b.tensor_copy(carry[:, :], aA[:, 8192 + 511::512]),
                            wait=scan_done)
                carry_init = v_carry
                dbg_store(f"S{p}", aA[:, 8192:16384], v_carry)

                q_ev = [None] * 16
                for c in range(16):
                    i4 = ring_next()
                    sl4, rd4 = ring.get(i4)
                    tmm = None
                    for kc in range(16):
                        tmm = T(lambda b, s_=sl4, cc=c, k_=kc: b.matmul(
                            bank(cc % 2), s_[:, 128 * k_:128 * k_ + 128],
                            xnf[:, 512 * k_:512 * k_ + 512],
                            start=(k_ == 0), stop=(k_ == 15)),
                            wait=[rd4, bank_free[c % 2], aB_free if c < 2 else None]
                            if kc == 0 else None)
                    ring.done(i4, (tsm, tsm.n))
                    q_ev[c] = V(lambda b, cc=c: b.tensor_copy(
                        aB[:, 512 * cc:512 * cc + 512], bank(cc % 2)), wait=tmm)
                    bank_free[c % 2] = q_ev[c]
                y_done = [None] * 16
                for c in range(16):
                    y_done[c] = V(lambda b, cc=c: b.tensor_tensor(
                        aB[:, 512 * cc:512 * cc + 512], aB[:, 512 * cc:512 * cc + 512],
                        aA[:, 8192 + 512 * cc:8192 + 512 * cc + 512], ALU.mult),
                        wait=[q_ev[c], scan_done if c == 0 else None])
                dbg_store(f"y{p}", aB[:, 0:8192], y_done[-1])

                gate_ev = [None] * 16
                for c in range(16):
                    i5 = ring_next()
                    sl5, rd5 = ring.get(i5)
                    tmm = None
                    for kc in range(16):
                        tmm = T(lambda b, s_=sl5, cc=c, k_=kc: b.matmul(
                            bank(cc % 2), s_[:, 128 * k_:128 * k_ + 128],
                            xnf[:, 512 * k_:512 * k_ + 512],
                            start=(k_ == 0), stop=(k_ == 15)),
                            wait=[rd5, bank_free[c % 2], y_done[c] if c < 16 else None]
                            if kc == 0 else None)
                    ring.done(i5, (tsm, tsm.n))
                    gate_ev[c] = S(lambda b, cc=c: b.activation(
                        aA[:, 8192 + 512 * cc:8192 + 512 * cc + 512], bank(cc % 2), AF.Silu),
                        wait=[tmm] + (dbgw() if c == 0 else []))
                    bank_free[c % 2] = gate_ev[c]

                t_sy = None
                for c in range(16):
                    t_sy = T(lambda b, cc=c: b.matmul(
                        psB[0:1, 0:512], onescol[:, :], aB[:, 512 * cc:512 * cc + 512],
                        start=(cc == 0), stop=(cc == 15)),
                        wait=[y_done[15], tr_free, pB_free] if c == 0 else None)
                sy2_dep = None
                for c in range(16):
                    sq_s = S(lambda b, cc=c: b.activation(
                        ysq[:, :], aB[:, 512 * cc:512 * cc + 512], AF.Square),
                        wait=[y_done[15], sqd_free] if c == 0 else sy2_dep)
                    sqd_free = sq_s
                    sy2_dep = T(lambda b, cc=c: b.matmul(
                        psB[32:33, 0:512], onescol[:, :], ysq[:, :],
                        start=(cc == 0), stop=(cc == 15)), wait=sq_s)
                v_r1 = V(lambda b: b.tensor_scalar(rowf[0:1, 0:512], psB[0:1, 0:512],
                                                   1.0 / D, None, ALU.mult), wait=[t_sy, sy2_dep])
                v_r2 = V(lambda b: b.tensor_scalar(rowf[0:1, 512:1024], psB[32:33, 0:512],
                                                   1.0 / D, None, ALU.mult))
                v_r3 = V(lambda b: b.tensor_tensor(rowf[0:1, 1024:1536], rowf[0:1, 0:512],
                                                   rowf[0:1, 0:512], ALU.mult))
                v_r4 = V(lambda b: b.tensor_tensor(rowf[0:1, 512:1024], rowf[0:1, 512:1024],
                                                   rowf[0:1, 1024:1536], ALU.subtract))
                s_r1 = S(lambda b: b.activation(rowf[0:1, 512:1024], rowf[0:1, 512:1024],
                                                AF.Ln, bias=consts[0:1, CEPS:CEPS + 1]), wait=v_r4)
                s_r2 = S(lambda b: b.activation(rowf[0:1, 512:1024], rowf[0:1, 512:1024],
                                                AF.Exp, scale=consts[0:1, CNH:CNH + 1]))
                v_rb = V(lambda b: b.tensor_copy(rowsb[0:1, 0:512], rowf[0:1, 0:512]), wait=s_r2)
                v_rb = V(lambda b: b.tensor_copy(rowsb[0:1, 512:1024], rowf[0:1, 512:1024]))
                t_b1 = T(lambda b: b.matmul(psB[:, 1024:1536], ones1r[:, :], rowsb[:, 0:512],
                                            start=True, stop=True), wait=v_rb)
                v_mb = V(lambda b: b.tensor_copy(mbt[:, :], psB[:, 1024:1536]), wait=t_b1)
                t_b2 = T(lambda b: b.matmul(psB[:, 1024:1536], ones1r[:, :], rowsb[:, 512:1024],
                                            start=True, stop=True), wait=v_mb)
                v_rsb = V(lambda b: b.tensor_copy(rsbt[:, :], psB[:, 1024:1536]), wait=t_b2)
                tr_free = v_rsb
                pB_free = v_rsb
                yg_done = None
                for c in range(16):
                    V(lambda b, cc=c: b.tensor_tensor(
                        ygs[:, :], aB[:, 512 * cc:512 * cc + 512], mbt[:, :], ALU.subtract),
                        wait=[v_mb, v_rsb, gate_ev[15]] + dbgw() if c == 0 else None)
                    V(lambda b, cc=c: b.tensor_tensor(ygs[:, :], ygs[:, :], rsbt[:, :], ALU.mult))
                    V(lambda b, cc=c: b.tensor_scalar(
                        ygs[:, :], ygs[:, :], lngc[:, cc:cc + 1], lnbc[:, cc:cc + 1],
                        ALU.mult, ALU.add))
                    yg_done = V(lambda b, cc=c: b.tensor_tensor(
                        xnf[:, 512 * cc:512 * cc + 512], ygs[:, :],
                        aA[:, 8192 + 512 * cc:8192 + 512 * cc + 512], ALU.mult),
                        wait=[gate_ev[c], xnf_ready if c == 0 else None])
                dbg_store(f"yg{p}", xnf[:, :], yg_done)
                xnf_free = None

                s_ev = None
                for n in range(8):
                    i6 = ring_next()
                    sl6, rd6 = ring.get(i6)
                    for m in range(NM):
                        wob = bank((n + m) % 2)
                        tmm = None
                        for dc in range(16):
                            tmm = T(lambda b, s_=sl6, n_=n, mm=m, d_=dc, bk=wob: b.matmul(
                                bk[:, 0:256],
                                xnf[:, 512 * d_ + 128 * mm:512 * d_ + 128 * mm + 128],
                                s_[:, 256 * d_:256 * d_ + 256],
                                start=(d_ == 0), stop=(d_ == 15)),
                                wait=([rd6, yg_done if (n == 0 and m == 0) else None,
                                       bank_free[(n + m) % 2]]
                                      + (dbgw() if (n == 0 and m == 0) else []))
                                if dc == 0 else None)
                        s_ev = V(lambda b, n_=n, mm=m, bk=wob: b.tensor_copy(
                            aB[:, 8192 + 2048 * mm + 256 * n_:8192 + 2048 * mm + 256 * n_ + 256],
                            bk[:, 0:256]),
                            wait=[tmm, gate_ev[15] if (n == 0 and m == 0) else None])
                        bank_free[(n + m) % 2] = s_ev
                    ring.done(i6, (tsm, tsm.n))
                wo_done = s_ev
                xnf_free = wo_done
                dbg_store(f"s{p}", aB[:, 8192:16384], wo_done)

                res_done = None
                for m in range(NM):
                    ld = GD(lambda b, mm=m, xb_=xb, r_=r0: b.dma_start(
                        out=xslot(mm), in_=xb_[r_ + 128 * mm:r_ + 128 * mm + 128, :]),
                        GX[m % 2], wait=xp_free[m % 2])
                    gsrc = ktms if m < 2 else kfs
                    vr = None
                    for cch in range(4):
                        V(lambda b, mm=m, cc=cch, gs_=gsrc: b.tensor_tensor(
                            ygs[:, :],
                            aB[:, 8192 + 2048 * mm + 512 * cc:8192 + 2048 * mm + 512 * cc + 512],
                            gs_[:, 2048 * mm + 512 * cc:2048 * mm + 512 * cc + 512], ALU.mult),
                            wait=([ld, wo_done, ffn_done] + dbgw()
                                  if (m == 0 and cch == 0)
                                  else ([ld] if cch == 0 else None)))
                        vr = V(lambda b, mm=m, cc=cch: b.tensor_tensor(
                            xslot(mm)[:, 512 * cc:512 * cc + 512],
                            xslot(mm)[:, 512 * cc:512 * cc + 512], ygs[:, :], ALU.add))
                    sv = SSD(lambda b, mm=m, bs=bsel, r_=r0: b.dma_start(
                        out=out_d[bs, r_ + 128 * mm:r_ + 128 * mm + 128, :], in_=xslot(mm)),
                        SO[m % 2], wait=vr)
                    xp_free[m % 2] = sv
                    res_done = sv
                dbg_store(f"gk{p}", ktms[:, :], ffn_done)
                dbg_store(f"fst{p}", st2[:, :], ffn_done)
                dbg_store("wc", wcols[:, 0:4], ffn_done)
                dbg_store(f"k2f{p}", aB[:, :], [ktr])
                dd = dbgw()
                kfs_free = [res_done] + dd
                kt_free = [[res_done] + dd, [res_done] + dd]
                aB_free = [res_done] + dd
                aA_free = [yg_done] + dd
                xnf_free = ([xnf_free] + dd) if not isinstance(xnf_free, list) else (xnf_free + dd)
                pA_free = bank_free[0]

        except _StopBuild:
            pass
        P.emit("sync", None, wait=[(s, s.n) for s in store_sems])

        with nc.Block() as block:
            @block.sync
            def _(b):
                P.replay("sync", b)

            @block.gpsimd
            def _(b):
                P.replay("gpsimd", b)

            @block.vector
            def _(b):
                P.replay("vector", b)

            @block.scalar
            def _(b):
                P.replay("scalar", b)

            @block.tensor
            def _(b):
                P.replay("tensor", b)

    return nc


def _host_consts():
    ident = np.eye(128).astype(ml_dtypes.bfloat16)
    ident16 = np.zeros((128, 16), dtype=np.float32)
    ident16[0:16] = np.eye(16)
    ident16[32:48] = np.eye(16)
    ident16 = ident16.astype(ml_dtypes.bfloat16)
    iotacol = (127.0 - np.arange(128, dtype=np.float32)).reshape(128, 1)
    return ident, ident16, iotacol


def make_in_maps(inputs, n_passes=NPASS, dbg=()):
    x = np.ascontiguousarray(np.asarray(inputs["L0_attn_x"], dtype=np.float32))
    Wq = np.ascontiguousarray(np.asarray(inputs["L0_attn_Wq"], dtype=np.float32))
    Wv = np.ascontiguousarray(np.asarray(inputs["L0_attn_Wv"], dtype=np.float32))
    Wg = np.ascontiguousarray(np.asarray(inputs["L0_attn_Wg"], dtype=np.float32))
    Wo = np.ascontiguousarray(np.asarray(inputs["L0_attn_Wo"], dtype=np.float32))
    beta = np.asarray(inputs["L0_attn_beta"], dtype=np.float32).reshape(16, 1)
    lng = np.asarray(inputs["L0_attn_ln_g"], dtype=np.float32).reshape(1, D)
    lnb = np.asarray(inputs["L0_attn_ln_b"], dtype=np.float32).reshape(1, D)
    w1 = np.ascontiguousarray(np.asarray(inputs["L0_ffn_w1"], dtype=np.float32))
    w2 = np.ascontiguousarray(np.asarray(inputs["L0_ffn_w2"], dtype=np.float32))
    ident, ident16, iotacol = _host_consts()
    in_maps = []
    for k in range(NCORE):
        b0 = k
        b1 = 7 - k
        pref = np.zeros((PREF, D), dtype=np.float32)
        n0 = b0 * SLICE
        n1 = b1 * SLICE
        pref[0:n0] = x[0, 0:n0]
        pref[n0:n0 + n1] = x[1, 0:n1]
        onehot = np.zeros((128, PGRP), dtype=np.float32)
        gsel = 2 * k - 1
        if gsel >= 0:
            onehot[:, gsel] = 1.0
        nb = np.full((128, 1), float(PGRP - 2 * k), dtype=np.float32)
        in_maps.append(dict(
            x0=np.ascontiguousarray(x[0, n0:n0 + SLICE]),
            x1=np.ascontiguousarray(x[1, n1:n1 + SLICE]),
            xpref=pref,
            Wq=Wq, Wv=Wv, Wg=Wg, Wo=Wo, w1=w1, w2=w2,
            beta16=beta, lng_r=lng, lnb_r=lnb,
            ident=ident, ident16=ident16, iotacol=iotacol,
            onehot=onehot, nbound=nb,
        ))
    return in_maps


def kernel(**inputs):
    nc = build()
    in_maps = make_in_maps(inputs)
    res = run_bass_kernel_spmd(nc, in_maps, core_ids=list(range(NCORE)))
    x = np.asarray(inputs["L0_attn_x"], dtype=np.float32)
    out = np.empty((2, 16384, D), dtype=np.float32)
    for k in range(NCORE):
        o = np.asarray(res.results[k]["out"])
        out[0, k * SLICE:(k + 1) * SLICE] = o[0]
        out[1, (7 - k) * SLICE:(8 - k) * SLICE] = o[1]
    return out
